# version 02
# baseline (speedup 1.0000x reference)
import numpy as np
from contextlib import ExitStack

import concourse.bass as bass
import concourse.mybir as mybir
from concourse.bass_utils import run_bass_kernel_spmd

F32 = mybir.dt.float32
BF16 = mybir.dt.bfloat16
AF = mybir.ActivationFunctionType
ALU = mybir.AluOpType

NCORES = 8
S = 2
T = 2048
H = 2048
HC = H // 128
TC = T // 128
NT5 = T // 512
E = 8
O = 2048
EPS = 1e-5
NRING = 10

DBG = ()


class Sem:
    __slots__ = ("h", "n")

    def __init__(self, es, nc, name):
        self.h = es.enter_context(nc.semaphore(name))
        self.n = 0

    def inc(self, instr, k=1):
        instr.then_inc(self.h, k)
        self.n += k
        return instr


class DSem:

    def __init__(self, es, nc, name, nway=2, m=1):
        self.s = [Sem(es, nc, f"{name}_{i}") for i in range(nway)]
        self.nway = nway
        self.m = m
        self.slots = 0

    def inc(self, instr, slot):
        self.s[slot % self.nway].inc(instr, 16)

    def wait_slots(self, eng, n):
        for j in range(self.nway):
            cnt = (n - j + self.nway - 1) // self.nway
            if cnt > 0:
                eng.wait_ge(self.s[j].h, cnt * self.m * 16)

    def wait_all(self, eng):
        for j in range(self.nway):
            if self.s[j].n:
                eng.wait_ge(self.s[j].h, self.s[j].n)


class EngChain:
    def __init__(self, es, nc, name):
        self.sem = Sem(es, nc, name)
        self.last = None


class CE:

    def __init__(self, eng, chain):
        self._e = eng
        self._c = chain

    def wait_ge(self, h, v):
        self._e.wait_ge(h, v)

    def dma_start(self, **kw):
        return self._e.dma_start(**kw)

    def _pre(self):
        if self._c.last is not None:
            self._e.wait_ge(self._c.last[0], self._c.last[1])

    def _post(self, ins, inc, inc_k):
        if inc is not None:
            inc.inc(ins, inc_k)
            self._c.last = (inc.h, inc.n)
        else:
            s = self._c.sem
            s.inc(ins, 1)
            self._c.last = (s.h, s.n)
        return ins

    def tensor_tensor(self, out, a, b, op, inc=None, inc_k=1):
        self._pre()
        return self._post(self._e.tensor_tensor(out, a, b, op), inc, inc_k)

    def tensor_scalar(self, out, a, s1, s2, op0, op1, accum_out=None, inc=None, inc_k=1):
        self._pre()
        return self._post(self._e.tensor_scalar(out, a, s1, s2, op0, op1,
                                                accum_out=accum_out), inc, inc_k)

    def scalar_tensor_tensor(self, out, in0, scalar, in1, op0, op1, inc=None, inc_k=1):
        self._pre()
        return self._post(self._e.scalar_tensor_tensor(out, in0, scalar, in1, op0, op1),
                          inc, inc_k)

    def tensor_tensor_scan(self, out, d0, d1, init, op0, op1, inc=None, inc_k=1):
        self._pre()
        return self._post(self._e.tensor_tensor_scan(out, d0, d1, init, op0, op1),
                          inc, inc_k)

    def activation(self, out, in_, func, bias=None, scale=None, accum_out=None,
                   inc=None, inc_k=1):
        kw = {}
        if bias is not None:
            kw["bias"] = bias
        if scale is not None:
            kw["scale"] = scale
        if accum_out is not None:
            kw["accum_out"] = accum_out
        self._pre()
        return self._post(self._e.activation(out, in_, func, **kw), inc, inc_k)

    def nop(self, inc=None, inc_k=1):
        self._pre()
        return self._post(self._e.nop(), inc, inc_k)


def build(dbg=(), stop=None):
    nc = bass.Bass("TRN2", target_bir_lowering=False, debug=False)
    ES = ExitStack()

    def sbt(name, shape, dt):
        return ES.enter_context(nc.sbuf_tensor(name, list(shape), dt))

    def sem(name):
        return Sem(ES, nc, name)

    def din(name, shape, dt=F32):
        return nc.dram_tensor(name, list(shape), dt, kind="ExternalInput").ap()

    def dscr(name, shape, dt):
        kind = "ExternalOutput" if name in dbg else "Internal"
        return nc.dram_tensor(name, list(shape), dt, kind=kind).ap()

    x_hbm = din("x", [S, T, H])
    wk_h = din("wk", [HC, 128, HC, 128])
    wv_h = din("wv", [HC, 128, HC, 128])
    wr_h = din("wr", [HC, 128, HC, 128])
    wo_h = din("wo", [H, H])
    wrt_h = din("wrt", [E, HC, 128, HC, 128])
    rb_h = din("rb", [128, 128])
    we_h = din("we", [E, 4, 128, HC, 512])
    eb_h = din("eb", [8, O])
    td_h = din("td", [128, HC])
    tf_h = din("tf", [128, HC])
    tmk_h = din("tmk", [128, HC])
    tmv_h = din("tmv", [128, HC])
    tmr_h = din("tmr", [128, HC])
    eye_h = din("eye", [128, 128])

    out_hbm = nc.dram_tensor("out", [S, T, H], F32, kind="ExternalOutput").ap()

    a_tm = dscr("a_tm", [S, T, H], BF16)
    u_sc = dscr("u_sc", [S, HC, 128, T], BF16)
    x1_sc = dscr("x1_sc", [S, T, H], F32)
    b_tm = dscr("b_tm", [S, T, H], BF16)
    ffn_sc = dscr("ffn_sc", [S, TC, 128, O], F32)
    c_dbg = dscr("c_dbg", [S, 128, 128], F32) if "c_dbg" in dbg else None
    kin_dbg = dscr("kin_dbg", [NT5, 128, HC * 512], BF16) if "kin_dbg" in dbg else None
    kpr_dbg = dscr("kpr_dbg", [NT5, 2, 128, 512], F32) if "kin_dbg" in dbg else None

    g_td = sbt("g_td", [128, HC], F32)
    g_tf = sbt("g_tf", [128, HC], F32)
    g_tmk = sbt("g_tmk", [128, HC], F32)
    g_tmv = sbt("g_tmv", [128, HC], F32)
    g_tmr = sbt("g_tmr", [128, HC], F32)
    g_w = sbt("g_w", [128, HC], F32)
    g_etf = sbt("g_etf", [128, HC], F32)
    g_rb = sbt("g_rb", [128, 128], F32)
    g_eb = sbt("g_eb", [8, O], F32)
    g_eye = sbt("g_eye", [128, 128], F32)
    g_ones512 = sbt("g_ones512", [128, 512], F32)
    g_onescol = sbt("g_onescol", [128, 1], BF16)
    g_zerocol = sbt("g_zerocol", [128, 1], BF16)
    g_eps = sbt("g_eps", [128, 1], F32)

    LD = sem("LD")
    GM = sem("GM")
    CA = sem("CA")
    CV = sem("CV")
    BAR = sem("BAR")

    vchain = EngChain(ES, nc, "VSELF")
    schain = EngChain(ES, nc, "ASELF")
    blk = ES.enter_context(nc.Block())

    @blk.sync
    def _(sy):
        for dst, src in [(g_td, td_h), (g_tf, tf_h), (g_tmk, tmk_h), (g_tmv, tmv_h),
                         (g_tmr, tmr_h), (g_rb, rb_h), (g_eb, eb_h), (g_eye, eye_h)]:
            LD.inc(sy.dma_start(out=dst[:], in_=src[:, :]), 16)
    ld_const = LD.n

    @blk.gpsimd
    def _(g):
        g.memset(g_ones512[:], 1.0)
        g.memset(g_onescol[:], 1.0)
        g.memset(g_zerocol[:], 0.0)
        GM.inc(g.memset(g_eps[:], EPS), 1)

    @blk.scalar
    def _(sc):
        sc = CE(sc, schain)
        sc.wait_ge(LD.h, ld_const)
        sc.activation(g_w[:], g_td[:], AF.Exp)
        sc.activation(g_etf[:], g_tf[:], AF.Exp, inc=CA, inc_k=1)

    @blk.vector
    def _(ve):
        ve = CE(ve, vchain)
        ve.wait_ge(CA.h, 1)
        ve.wait_ge(GM.h, 1)
        ve.tensor_scalar(g_w[:], g_w[:], -1.0, None, ALU.mult, ALU.bypass, inc=CV, inc_k=1)

    aes = ExitStack()

    def asb(name, shape, dt):
        return aes.enter_context(nc.sbuf_tensor(name, list(shape), dt))

    def apsum(name):
        return aes.enter_context(nc.psum_tensor(name, [128, 512], F32))

    pa_x = [asb(f"pa_x{i}", [128, H], F32) for i in range(2)]
    pa_scr = asb("pa_scr", [128, H], BF16)
    pa_st = asb("pa_st", [128, 8], F32)
    pa_a = [asb(f"pa_a{i}", [128, H], BF16) for i in range(2)]

    pb_af = [asb(f"pb_af{i}", [128, HC * 512], BF16) for i in range(2)]
    pb_kin = asb("pb_kin", [128, HC * 512], BF16)
    pb_vin = asb("pb_vin", [128, HC * 512], BF16)
    pb_rin = asb("pb_rin", [128, HC * 512], BF16)
    pb_d = asb("pb_d", [128, 512], F32)
    pb_wt = [asb(f"pb_wt{i}", [128, 3 * H], BF16) for i in range(2)]
    pb_k = [asb(f"pb_k{i}", [128, 512], F32) for i in range(2)]
    pb_v = [asb(f"pb_v{i}", [128, 512], F32) for i in range(2)]
    pb_E = [asb(f"pb_E{i}", [128, 512], F32) for i in range(2)]
    pb_wbuf = asb("pb_wbuf", [128, 512], F32)
    pb_mx = asb("pb_mx", [128, 512], F32)
    pb_d1 = asb("pb_d1", [128, 512], F32)
    pb_d2 = asb("pb_d2", [128, 512], F32)
    pb_e1s = asb("pb_e1s", [128, 512], F32)
    pb_alpha = asb("pb_alpha", [128, 512], F32)
    pb_e2s = asb("pb_e2s", [128, 512], F32)
    pb_e2v = asb("pb_e2v", [128, 512], F32)
    pb_num = asb("pb_num", [128, 512], F32)
    pb_den = asb("pb_den", [128, 512], F32)
    pb_t1 = asb("pb_t1", [128, 512], F32)
    pb_NUM = asb("pb_NUM", [128, 512], F32)
    pb_t3 = asb("pb_t3", [128, 512], F32)
    pb_DEN = asb("pb_DEN", [128, 512], F32)
    pb_D2 = asb("pb_D2", [128, 512], F32)
    pb_ln = asb("pb_ln", [128, 512], F32)
    pb_r2 = asb("pb_r2", [128, 512], F32)
    pb_u = [asb(f"pb_u{i}", [128, 512], BF16) for i in range(2)]
    pb_mxs = [asb(f"pb_mxs{i}", [128, 1], F32) for i in range(HC)]
    pb_ns = [asb(f"pb_ns{i}", [128, 1], F32) for i in range(HC)]
    pb_ds = [asb(f"pb_ds{i}", [128, 1], F32) for i in range(HC)]

    ps_k = [apsum(f"ps_k{i}") for i in range(2)]
    ps_v = [apsum(f"ps_v{i}") for i in range(2)]
    ps_r = [apsum(f"ps_r{i}") for i in range(2)]

    AXL = DSem(ES, nc, "AXL")
    AS1 = sem("AS1"); AV1 = sem("AV1"); AS2 = sem("AS2")
    AV2 = sem("AV2"); AAP = sem("AAP")
    AST = DSem(ES, nc, "AST")
    TLD = DSem(ES, nc, "TLD", m=HC)
    MIXD = sem("MIXD")
    WLD = DSem(ES, nc, "WLD", m=3)
    KMM = sem("KMM"); VMM = sem("VMM"); RMM = sem("RMM")
    KEV = sem("KEV"); VEV = sem("VEV"); REV = sem("REV")
    SDA = sem("SDA"); SAD = sem("SAD"); UDONE = sem("UDONE")
    UST = DSem(ES, nc, "UST")
    WOLD = sem("WOLD")
    ULD = DSem(ES, nc, "ULD")
    CXL = DSem(ES, nc, "CXL")
    OMM = sem("OMM"); OEV = sem("OEV"); OSQ = sem("OSQ")
    CV1 = sem("CV1"); CS1 = sem("CS1"); CV2 = sem("CV2"); CS2 = sem("CS2")
    CV3 = sem("CV3"); CS3 = sem("CS3"); CV4 = sem("CV4"); CAPP = sem("CAPP")
    DBGS = sem("DBGS")
    X1ST = DSem(ES, nc, "X1ST")
    BST = DSem(ES, nc, "BST")

    for s in range(S):
        @blk.sync
        def _(sy, s=s):
            for tc in range(TC):
                it = s * TC + tc
                if it >= 2:
                    sy.wait_ge(AAP.h, it - 1)
                AXL.inc(sy.dma_start(out=pa_x[it % 2][:],
                                     in_=x_hbm[s, tc * 128:(tc + 1) * 128, :]), it)
                if tc >= 1:
                    jt = it - 1
                    sy.wait_ge(AAP.h, jt + 1)
                    AST.inc(sy.dma_start(out=a_tm[s, (jt % TC) * 128:((jt % TC) + 1) * 128, :],
                                         in_=pa_a[jt % 2][:]), jt)
            jt = s * TC + TC - 1
            sy.wait_ge(AAP.h, jt + 1)
            AST.inc(sy.dma_start(out=a_tm[s, (jt % TC) * 128:((jt % TC) + 1) * 128, :],
                                 in_=pa_a[jt % 2][:]), jt)

        @blk.scalar
        def _(sc, s=s):
            sc = CE(sc, schain)
            for tc in range(TC):
                it = s * TC + tc
                xb = pa_x[it % 2]
                AXL.wait_slots(sc, it + 1)
                sc.activation(pa_scr[:], xb[:], AF.Identity, accum_out=pa_st[:, 0:1])
                sc.activation(pa_scr[:], xb[:], AF.Square, accum_out=pa_st[:, 1:2], inc=AS1, inc_k=1)
                sc.wait_ge(AV1.h, it + 1)
                sc.activation(pa_st[:, 4:5], pa_st[:, 3:4], AF.Ln, bias=g_eps[:, 0:1])
                sc.activation(pa_st[:, 5:6], pa_st[:, 4:5], AF.Exp, scale=-0.5, inc=AS2, inc_k=1)
                sc.wait_ge(AV2.h, it + 1)
                if it >= 2:
                    AST.wait_slots(sc, it - 1)
                sc.activation(pa_a[it % 2][:], xb[:], AF.Identity,
                                      bias=pa_st[:, 6:7], scale=pa_st[:, 5:6], inc=AAP, inc_k=1)

        @blk.vector
        def _(ve, s=s):
            ve = CE(ve, vchain)
            for tc in range(TC):
                it = s * TC + tc
                ve.wait_ge(AS1.h, it + 1)
                ve.tensor_scalar(pa_st[:, 2:3], pa_st[:, 0:1], 1.0 / H, None, ALU.mult, ALU.bypass)
                ve.tensor_scalar(pa_st[:, 7:8], pa_st[:, 1:2], 1.0 / H, None, ALU.mult, ALU.bypass)
                ve.tensor_tensor(pa_st[:, 3:4], pa_st[:, 2:3], pa_st[:, 2:3], ALU.mult)
                ve.tensor_tensor(pa_st[:, 3:4], pa_st[:, 7:8], pa_st[:, 3:4], ALU.subtract, inc=AV1, inc_k=1)
                ve.wait_ge(AS2.h, it + 1)
                ve.scalar_tensor_tensor(pa_st[:, 6:7], pa_st[:, 2:3], -1.0,
                                                pa_st[:, 5:6], ALU.mult, ALU.mult, inc=AV2, inc_k=1)


    if stop == "A":
        @blk.sync
        def _(sy):
            AST.wait_all(sy)
        aes.close()
        ES.close()
        return nc

    for s in range(S):
        @blk.sync
        def _(sy, s=s):
            for t5 in range(NT5):
                it = s * NT5 + t5
                AST.wait_slots(sy, s * TC + (t5 + 1) * 4)
                if it >= 2:
                    sy.wait_ge(MIXD.h, it)
                for hh in range(HC):
                    TLD.inc(sy.dma_start(
                        out=pb_af[it % 2][:, hh * 512:(hh + 1) * 512],
                        in_=a_tm[s, t5 * 512:(t5 + 1) * 512, hh * 128:(hh + 1) * 128],
                        transpose=True), it)

        @blk.gpsimd
        def _(g, s=s):
            def ustore(g, jt):
                t5j = (jt // HC) % NT5
                acj = jt % HC
                g.wait_ge(UDONE.h, jt + 1)
                UST.inc(g.dma_start(out=u_sc[s, acj, :, t5j * 512:(t5j + 1) * 512],
                                    in_=pb_u[jt % 2][:]), jt)
            for t5 in range(NT5):
                for ac in range(HC):
                    lc = t5 * HC + ac
                    it = s * NT5 * HC + lc
                    if it >= 2:
                        g.wait_ge(RMM.h, it - 1)
                    WLD.inc(g.dma_start(out=pb_wt[it % 2][:, 0:H], in_=wk_h[ac, :, :, :]), it)
                    WLD.inc(g.dma_start(out=pb_wt[it % 2][:, H:2 * H], in_=wv_h[ac, :, :, :]), it)
                    WLD.inc(g.dma_start(out=pb_wt[it % 2][:, 2 * H:3 * H], in_=wr_h[ac, :, :, :]), it)
                    if lc >= 1:
                        ustore(g, it - 1)
                    if kin_dbg is not None and s == 0 and ac == HC - 1:
                        g.wait_ge(MIXD.h, t5 + 1)
                        DBGS.inc(g.dma_start(out=kin_dbg[t5, :, :], in_=pb_kin[:]), 16)
                        itc0 = t5 * HC
                        g.wait_ge(KEV.h, itc0 + 1)
                        DBGS.inc(g.dma_start(out=kpr_dbg[t5, 0, :, :], in_=pb_k[itc0 % 2][:]), 16)
                        g.wait_ge(VEV.h, itc0 + 1)
                        DBGS.inc(g.dma_start(out=kpr_dbg[t5, 1, :, :], in_=pb_v[itc0 % 2][:]), 16)
            ustore(g, s * NT5 * HC + NT5 * HC - 1)

        @blk.vector
        def _(ve, s=s):
            ve = CE(ve, vchain)
            if s == 0:
                ve.wait_ge(GM.h, 1)
            for t5 in range(NT5):
                it = s * NT5 + t5
                TLD.wait_slots(ve, it + 1)
                if it >= 1:
                    ve.wait_ge(RMM.h, it * HC)
                cur = pb_af[it % 2]
                prv = pb_af[(it - 1) % 2]
                for hh in range(HC):
                    c0 = cur[:, hh * 512:hh * 512 + 1]
                    cm = cur[:, hh * 512 + 1:(hh + 1) * 512]
                    cm0 = cur[:, hh * 512:(hh + 1) * 512 - 1]
                    shift0 = g_zerocol[:, 0:1] if t5 == 0 else prv[:, hh * 512 + 511:hh * 512 + 512]
                    ve.tensor_tensor(pb_d[:, 1:512], cm, cm0, ALU.subtract)
                    ve.tensor_tensor(pb_d[:, 0:1], c0, shift0, ALU.subtract)
                    for mi, (mix, tmcol) in enumerate(((pb_kin, g_tmk), (pb_vin, g_tmv),
                                                       (pb_rin, g_tmr))):
                        om = mix[:, hh * 512:(hh + 1) * 512]
                        ve.scalar_tensor_tensor(om[:, 1:512], pb_d[:, 1:512],
                                                tmcol[:, hh:hh + 1], cm0, ALU.mult, ALU.add)
                        ve.scalar_tensor_tensor(om[:, 0:1], pb_d[:, 0:1],
                                                tmcol[:, hh:hh + 1], shift0, ALU.mult, ALU.add,
                                                inc=MIXD if (hh == HC - 1 and mi == 2) else None)
                for ac in range(HC):
                    itc = it * HC + ac
                    par = itc % 2
                    ve.wait_ge(KEV.h, itc + 1)
                    ve.tensor_scalar(pb_wbuf[:], g_ones512[:], g_w[:, ac:ac + 1], None,
                                     ALU.mult, ALU.bypass)
                    ve.tensor_tensor_scan(pb_mx[:], pb_wbuf[:], pb_k[par][:],
                                          pb_mxs[ac][:] if t5 > 0 else -1e38,
                                          ALU.add, ALU.max)
                    if t5 > 0:
                        ve.tensor_tensor(pb_d1[:, 0:1], pb_mxs[ac][:], pb_mx[:, 0:1],
                                         ALU.subtract)
                    else:
                        ve.tensor_scalar(pb_d1[:, 0:1], pb_mx[:, 0:1], -1.0, -1e38,
                                         ALU.mult, ALU.add)
                    ve.tensor_tensor(pb_d1[:, 1:512], pb_mx[:, 0:511], pb_mx[:, 1:512],
                                     ALU.subtract)
                    ve.tensor_tensor(pb_d2[:], pb_k[par][:], pb_mx[:], ALU.subtract, inc=SDA, inc_k=1)
                    ve.wait_ge(SAD.h, itc * 2 + 1)
                    ve.wait_ge(VEV.h, itc + 1)
                    ve.tensor_tensor(pb_e2v[:], pb_e2s[:], pb_v[par][:], ALU.mult)
                    ve.tensor_tensor_scan(pb_num[:], pb_e1s[:], pb_e2v[:],
                                          pb_ns[ac][:] if t5 > 0 else 0.0,
                                          ALU.mult, ALU.add)
                    ve.tensor_tensor_scan(pb_den[:], pb_e1s[:], pb_e2s[:],
                                          pb_ds[ac][:] if t5 > 0 else 0.0,
                                          ALU.mult, ALU.add)
                    ve.tensor_scalar(pb_e2v[:], pb_e2v[:], g_etf[:, ac:ac + 1], None,
                                     ALU.mult, ALU.bypass)
                    ve.tensor_scalar(pb_e2s[:], pb_e2s[:], g_etf[:, ac:ac + 1], None,
                                     ALU.mult, ALU.bypass)
                    if t5 > 0:
                        ve.tensor_tensor(pb_t1[:, 0:1], pb_alpha[:, 0:1], pb_ns[ac][:],
                                         ALU.mult)
                    else:
                        ve.tensor_scalar(pb_t1[:, 0:1], pb_alpha[:, 0:1], 0.0, None,
                                         ALU.mult, ALU.bypass)
                    ve.tensor_tensor(pb_t1[:, 1:512], pb_alpha[:, 1:512], pb_num[:, 0:511],
                                     ALU.mult)
                    ve.tensor_tensor(pb_NUM[:], pb_t1[:], pb_e2v[:], ALU.add)
                    if t5 > 0:
                        ve.tensor_tensor(pb_t3[:, 0:1], pb_alpha[:, 0:1], pb_ds[ac][:],
                                         ALU.mult)
                    else:
                        ve.tensor_scalar(pb_t3[:, 0:1], pb_alpha[:, 0:1], 0.0, None,
                                         ALU.mult, ALU.bypass)
                    ve.tensor_tensor(pb_t3[:, 1:512], pb_alpha[:, 1:512], pb_den[:, 0:511],
                                     ALU.mult)
                    ve.tensor_tensor(pb_DEN[:], pb_t3[:], pb_e2s[:], ALU.add)
                    ve.tensor_scalar(pb_mxs[ac][:], pb_mx[:, 511:512], 1.0, None,
                                     ALU.mult, ALU.bypass)
                    ve.tensor_scalar(pb_ns[ac][:], pb_num[:, 511:512], 1.0, None,
                                     ALU.mult, ALU.bypass)
                    ve.tensor_scalar(pb_ds[ac][:], pb_den[:, 511:512], 1.0, None,
                                     ALU.mult, ALU.bypass)
                    ve.wait_ge(REV.h, itc + 1)
                    ve.scalar_tensor_tensor(pb_D2[:], pb_E[par][:], 1.0, pb_DEN[:],
                                                    ALU.add, ALU.mult, inc=SDA, inc_k=1)
                    ve.wait_ge(SAD.h, itc * 2 + 2)
                    if itc >= 2:
                        UST.wait_slots(ve, itc - 1)
                    ve.tensor_tensor(pb_u[par][:], pb_NUM[:], pb_r2[:], ALU.mult, inc=UDONE, inc_k=1)

        @blk.tensor
        def _(te, s=s):
            for t5 in range(NT5):
                it5 = s * NT5 + t5
                te.wait_ge(MIXD.h, it5 + 1)
                for ac in range(HC):
                    itc = it5 * HC + ac
                    WLD.wait_slots(te, itc + 1)
                    wt = pb_wt[itc % 2]
                    if itc >= 2:
                        te.wait_ge(KEV.h, itc - 1)
                        te.wait_ge(VEV.h, itc - 1)
                        te.wait_ge(REV.h, itc - 1)
                    for off, mix, pse, msem in ((0, pb_kin, ps_k, KMM), (H, pb_vin, ps_v, VMM),
                                                (2 * H, pb_rin, ps_r, RMM)):
                        for hh in range(HC):
                            ins = te.matmul(pse[itc % 2][:],
                                            wt[:, off + hh * 128:off + (hh + 1) * 128],
                                            mix[:, hh * 512:(hh + 1) * 512],
                                            start=(hh == 0), stop=(hh == HC - 1))
                            if hh == HC - 1:
                                msem.inc(ins, 1)

        @blk.scalar
        def _(sc, s=s):
            sc = CE(sc, schain)
            if s == 0:
                sc.wait_ge(CV.h, 1)
            for t5 in range(NT5):
                for ac in range(HC):
                    itc = (s * NT5 + t5) * HC + ac
                    par = itc % 2
                    if itc >= 2:
                        sc.wait_ge(SDA.h, 2 * (itc - 2) + 2)
                    sc.wait_ge(KMM.h, itc + 1)
                    sc.activation(pb_k[par][:], ps_k[par][:], AF.Identity, inc=KEV, inc_k=1)
                    sc.wait_ge(VMM.h, itc + 1)
                    sc.activation(pb_v[par][:], ps_v[par][:], AF.Identity, inc=VEV, inc_k=1)
                    sc.wait_ge(RMM.h, itc + 1)
                    sc.activation(pb_E[par][:], ps_r[par][:], AF.Exp, scale=-1.0, inc=REV, inc_k=1)
                    sc.wait_ge(SDA.h, itc * 2 + 1)
                    sc.activation(pb_e1s[:], pb_d1[:], AF.Exp, bias=g_w[:, ac:ac + 1])
                    sc.activation(pb_alpha[:], pb_d1[:], AF.Exp)
                    sc.activation(pb_e2s[:], pb_d2[:], AF.Exp, inc=SAD, inc_k=1)
                    sc.wait_ge(SDA.h, itc * 2 + 2)
                    sc.activation(pb_ln[:], pb_D2[:], AF.Ln)
                    sc.activation(pb_r2[:], pb_ln[:], AF.Exp, scale=-1.0, inc=SAD, inc_k=1)


    if stop == "B":
        @blk.sync
        def _(sy):
            UST.wait_all(sy)
        aes.close()
        ES.close()
        return nc

    @blk.vector
    def _(ve):
        ve = CE(ve, vchain)
        ve.nop(inc=BAR)
    @blk.scalar
    def _(sc):
        sc = CE(sc, schain)
        sc.nop(inc=BAR)
    @blk.tensor
    def _(te):
        BAR.inc(te.nop(), 1)
    @blk.sync
    def _(sy):
        UST.wait_all(sy)
        AST.wait_all(sy)
        BAR.inc(sy.nop(), 1)
    @blk.gpsimd
    def _(g):
        BAR.inc(g.nop(), 1)
    bar1 = BAR.n
    aes.close()

    ces = ExitStack()

    def csb(name, shape, dt):
        return ces.enter_context(nc.sbuf_tensor(name, list(shape), dt))

    pc_wo = csb("pc_wo", [128, HC * 2048], BF16)
    pc_u = [csb(f"pc_u{i}", [128, HC * 128], BF16) for i in range(2)]
    pc_x = [csb(f"pc_x{i}", [128, H], F32) for i in range(2)]
    pc_ao = csb("pc_ao", [128, H], F32)
    pc_x1 = [csb(f"pc_x1{i}", [128, H], F32) for i in range(2)]
    pc_b = [csb(f"pc_b{i}", [128, H], BF16) for i in range(2)]
    pc_scr = csb("pc_scr", [128, H], BF16)
    pc_st = csb("pc_st", [128, 24], F32)

    ps_o = [ces.enter_context(nc.psum_tensor(f"ps_o{i}", [128, 512], F32)) for i in range(2)]

    @blk.gpsimd
    def _(g):
        g.wait_ge(BAR.h, bar1)
        for ac in range(HC):
            WOLD.inc(g.dma_start(out=pc_wo[:, ac * 2048:(ac + 1) * 2048],
                                 in_=wo_h[ac * 128:(ac + 1) * 128, :]), 16)

    for s in range(S):
        @blk.sync
        def _(sy, s=s):
            sy.wait_ge(BAR.h, bar1)
            UST.wait_slots(sy, (s + 1) * NT5 * HC)
            def cstores(sy, jt):
                tcj = jt % TC
                sy.wait_ge(CV3.h, jt + 1)
                X1ST.inc(sy.dma_start(out=x1_sc[s, tcj * 128:(tcj + 1) * 128, :],
                                      in_=pc_x1[jt % 2][:]), jt)
                sy.wait_ge(CAPP.h, jt + 1)
                BST.inc(sy.dma_start(out=b_tm[s, tcj * 128:(tcj + 1) * 128, :],
                                     in_=pc_b[jt % 2][:]), jt)
            for tc in range(TC):
                it = s * TC + tc
                if it >= 2:
                    sy.wait_ge(OMM.h, (it - 1) * 4)
                ULD.inc(sy.dma_start(
                    out=pc_u[it % 2][:],
                    in_=u_sc[s, :, :, tc * 128:(tc + 1) * 128].rearrange("a p t -> p a t")), it)
                if it >= 2:
                    sy.wait_ge(CV2.h, it - 1)
                CXL.inc(sy.dma_start(out=pc_x[it % 2][:],
                                     in_=x_hbm[s, tc * 128:(tc + 1) * 128, :]), it)
                if tc >= 1:
                    cstores(sy, it - 1)
            cstores(sy, s * TC + TC - 1)

        @blk.tensor
        def _(te, s=s):
            if s == 0:
                te.wait_ge(BAR.h, bar1)
                te.wait_ge(WOLD.h, HC * 16)
            for tc in range(TC):
                it = s * TC + tc
                ULD.wait_slots(te, it + 1)
                for h5 in range(4):
                    jt = it * 4 + h5
                    if jt >= 2:
                        te.wait_ge(OEV.h, jt - 1)
                    for ac in range(HC):
                        ins = te.matmul(ps_o[jt % 2][:], pc_u[it % 2][:, ac * 128:(ac + 1) * 128],
                                        pc_wo[:, ac * 2048 + h5 * 512:ac * 2048 + (h5 + 1) * 512],
                                        start=(ac == 0), stop=(ac == HC - 1))
                        if ac == HC - 1:
                            OMM.inc(ins, 1)

        @blk.vector
        def _(ve, s=s):
            ve = CE(ve, vchain)
            if s == 0:
                ve.wait_ge(BAR.h, bar1)
            for tc in range(TC):
                it = s * TC + tc
                if it >= 1:
                    ve.wait_ge(OSQ.h, it * 4)
                for h5 in range(4):
                    jt = it * 4 + h5
                    ve.wait_ge(OMM.h, jt + 1)
                    ve.tensor_scalar(pc_ao[:, h5 * 512:(h5 + 1) * 512], ps_o[jt % 2][:],
                                             1.0, None, ALU.mult, ALU.add,
                                             accum_out=pc_st[:, h5:h5 + 1], inc=OEV, inc_k=1)
                ve.wait_ge(OSQ.h, (it + 1) * 4)
                ve.tensor_tensor(pc_st[:, 8:9], pc_st[:, 0:1], pc_st[:, 1:2], ALU.add)
                ve.tensor_tensor(pc_st[:, 9:10], pc_st[:, 2:3], pc_st[:, 3:4], ALU.add)
                ve.tensor_tensor(pc_st[:, 8:9], pc_st[:, 8:9], pc_st[:, 9:10], ALU.add)
                ve.tensor_tensor(pc_st[:, 10:11], pc_st[:, 4:5], pc_st[:, 5:6], ALU.add)
                ve.tensor_tensor(pc_st[:, 11:12], pc_st[:, 6:7], pc_st[:, 7:8], ALU.add)
                ve.tensor_tensor(pc_st[:, 10:11], pc_st[:, 10:11], pc_st[:, 11:12], ALU.add)
                ve.tensor_scalar(pc_st[:, 8:9], pc_st[:, 8:9], 1.0 / H, None, ALU.mult, ALU.bypass)
                ve.tensor_scalar(pc_st[:, 10:11], pc_st[:, 10:11], 1.0 / H, None, ALU.mult, ALU.bypass)
                ve.tensor_tensor(pc_st[:, 9:10], pc_st[:, 8:9], pc_st[:, 8:9], ALU.mult)
                ve.tensor_tensor(pc_st[:, 9:10], pc_st[:, 10:11], pc_st[:, 9:10],
                                         ALU.subtract, inc=CV1, inc_k=1)
                ve.wait_ge(CS1.h, it + 1)
                ve.scalar_tensor_tensor(pc_st[:, 13:14], pc_st[:, 8:9], -1.0, pc_st[:, 12:13],
                                        ALU.mult, ALU.mult)
                CXL.wait_slots(ve, it + 1)
                x1t = pc_x1[it % 2]
                if it >= 2:
                    ve.wait_ge(CAPP.h, it - 1)
                    X1ST.wait_slots(ve, it - 1)
                ve.tensor_scalar(x1t[:], pc_x[it % 2][:], pc_st[:, 13:14], None, ALU.add, ALU.bypass)
                ve.scalar_tensor_tensor(x1t[:], pc_ao[:], pc_st[:, 12:13], x1t[:],
                                                ALU.mult, ALU.add, inc=CV2, inc_k=1)
                ve.tensor_scalar(pc_scr[:], x1t[:], 1.0, None, ALU.mult, ALU.add,
                                         accum_out=pc_st[:, 14:15], inc=CV3, inc_k=1)
                ve.wait_ge(CS2.h, it + 1)
                ve.tensor_scalar(pc_st[:, 16:17], pc_st[:, 14:15], 1.0 / H, None, ALU.mult, ALU.bypass)
                ve.tensor_scalar(pc_st[:, 17:18], pc_st[:, 15:16], 1.0 / H, None, ALU.mult, ALU.bypass)
                ve.tensor_tensor(pc_st[:, 18:19], pc_st[:, 16:17], pc_st[:, 16:17], ALU.mult)
                ve.tensor_tensor(pc_st[:, 18:19], pc_st[:, 17:18], pc_st[:, 18:19],
                                         ALU.subtract, inc=CV4, inc_k=1)
                ve.wait_ge(CS3.h, it + 1)
                ve.scalar_tensor_tensor(pc_st[:, 20:21], pc_st[:, 16:17], -1.0,
                                                pc_st[:, 19:20], ALU.mult, ALU.mult, inc=CV1, inc_k=1)

        @blk.scalar
        def _(sc, s=s):
            sc = CE(sc, schain)
            if s == 0:
                sc.wait_ge(BAR.h, bar1)
            for tc in range(TC):
                it = s * TC + tc
                for h5 in range(4):
                    jt = it * 4 + h5
                    sc.wait_ge(OEV.h, jt + 1)
                    sc.activation(pc_scr[:, h5 * 512:(h5 + 1) * 512],
                                          pc_ao[:, h5 * 512:(h5 + 1) * 512], AF.Square,
                                          accum_out=pc_st[:, 4 + h5:5 + h5], inc=OSQ, inc_k=1)
                sc.wait_ge(CV1.h, 2 * it + 1)
                sc.activation(pc_st[:, 11:12], pc_st[:, 9:10], AF.Ln, bias=g_eps[:, 0:1])
                sc.activation(pc_st[:, 12:13], pc_st[:, 11:12], AF.Exp, scale=-0.5, inc=CS1, inc_k=1)
                sc.wait_ge(CV3.h, it + 1)
                sc.activation(pc_scr[:], pc_x1[it % 2][:], AF.Square,
                                      accum_out=pc_st[:, 15:16], inc=CS2, inc_k=1)
                sc.wait_ge(CV4.h, it + 1)
                sc.activation(pc_st[:, 21:22], pc_st[:, 18:19], AF.Ln, bias=g_eps[:, 0:1])
                sc.activation(pc_st[:, 19:20], pc_st[:, 21:22], AF.Exp, scale=-0.5, inc=CS3, inc_k=1)
                sc.wait_ge(CV1.h, 2 * it + 2)
                if it >= 2:
                    BST.wait_slots(sc, it - 1)
                sc.activation(pc_b[it % 2][:], pc_x1[it % 2][:], AF.Identity,
                                       bias=pc_st[:, 20:21], scale=pc_st[:, 19:20], inc=CAPP, inc_k=1)


    if stop == "C":
        @blk.sync
        def _(sy):
            X1ST.wait_all(sy)
            BST.wait_all(sy)
        ces.close()
        ES.close()
        return nc

    @blk.vector
    def _(ve):
        ve = CE(ve, vchain)
        ve.nop(inc=BAR)
    @blk.scalar
    def _(sc):
        sc = CE(sc, schain)
        sc.nop(inc=BAR)
    @blk.tensor
    def _(te):
        BAR.inc(te.nop(), 1)
    @blk.sync
    def _(sy):
        X1ST.wait_all(sy)
        BST.wait_all(sy)
        UST.wait_all(sy)
        BAR.inc(sy.nop(), 1)
    @blk.gpsimd
    def _(g):
        BAR.inc(g.nop(), 1)
    bar0 = BAR.n
    ces.close()

    NSLOT1 = HC * NT5
    DELAY = 2
    NRING_E = 9

    fes = ExitStack()

    def psb(name, shape, dt):
        return fes.enter_context(nc.sbuf_tensor(name, list(shape), dt))

    b_fm = psb("b_fm", [128, HC * 2048], BF16)
    c_sb = psb("c_sb", [128, 128], F32)
    cT_sb = [psb(f"cT{i}", [8, 128], F32) for i in range(TC)]
    fsum_sb = psb("fsum_sb", [128, TC * 4], F32)
    fsq_sb = psb("fsq_sb", [128, TC * 4], F32)
    ps_c = fes.enter_context(nc.psum_tensor("ps_c", [128, 512], F32))
    ps_L = [fes.enter_context(nc.psum_tensor(f"ps_L{i}", [128, 512], F32)) for i in range(2)]
    ps_B = [fes.enter_context(nc.psum_tensor(f"ps_B{i}", [128, 512], F32)) for i in range(2)]
    ps_Y = [fes.enter_context(nc.psum_tensor(f"ps_Y{i}", [128, 512], F32)) for i in range(2)]

    BFL = sem("BFL")
    RWL = DSem(ES, nc, "RWL", nway=NRING_E)
    LMM = sem("LMM"); LEVA = sem("LEVA")
    EV1 = sem("EV1"); EA1 = sem("EA1"); EV2 = sem("EV2"); EA2 = sem("EA2")
    CTR = sem("CTR"); CMM = sem("CMM"); CEV = sem("CEV")
    WEL = DSem(ES, nc, "WEL", nway=3)
    BMM = sem("BMM"); BEVA = sem("BEVA"); CTEV = sem("CTEV")
    YMM = sem("YMM"); YEV = sem("YEV"); FST = sem("FST"); FSQ = sem("FSQ")
    TPD = sem("TPD"); EBL = sem("EBL")
    GXL = DSem(ES, nc, "GXL", m=2)
    GV1 = sem("GV1"); GS1 = sem("GS1")
    OUTS = DSem(ES, nc, "OUTS")
    FFST = sem("FFST")

    def emit_barrier():
        @blk.vector
        def _(ve):
            ve = CE(ve, vchain)
            ve.nop(inc=BAR)
        @blk.scalar
        def _(sc):
            sc = CE(sc, schain)
            sc.nop(inc=BAR)
        @blk.tensor
        def _(te):
            BAR.inc(te.nop(), 1)
        @blk.sync
        def _(sy):
            UST.wait_all(sy)
            X1ST.wait_all(sy)
            BST.wait_all(sy)
            sy.wait_ge(FFST.h, FFST.n)
            OUTS.wait_all(sy)
            BAR.inc(sy.nop(), 1)
        @blk.gpsimd
        def _(g):
            BAR.inc(g.nop(), 1)
        return BAR.n

    def emit_cmm(te, fl):
        te.wait_ge(CTR.h, fl + 1)
        if fl % NSLOT1 == 0 and fl > 0:
            te.wait_ge(CEV.h, fl // NSLOT1)
        ctr = ctr_sb[fl % 2]
        t5f = fl % NT5
        for e in range(E):
            for j in range(4):
                tcf = t5f * 4 + j
                col = tcf * E + e
                ins = te.matmul(ps_c[:, col:col + 1],
                                ctr[:, e * 512 + j * 128:e * 512 + (j + 1) * 128],
                                g_onescol[:],
                                start=(fl % NSLOT1 == 0 and e == 0 and j == 0),
                                stop=(fl % NSLOT1 == NSLOT1 - 1 and e == E - 1 and j == 3),
                                skip_group_check=True)
                if e == E - 1 and j == 3:
                    CMM.inc(ins, 1)

    for s in range(S):
        ees = ExitStack()

        def esb(name, shape, dt):
            return ees.enter_context(nc.sbuf_tensor(name, list(shape), dt))

        wr_ring = [esb(f"wr_ring{s}_{i}", [128, 2048], BF16) for i in range(NRING_E)]
        L_sb = [esb(f"L_sb{s}_{i}", [128, E * 512], F32) for i in range(2)]
        ctr_sb = [esb(f"ctr{s}_{i}", [128, E * 512], BF16) for i in range(2)]
        d8_sb = esb(f"d8_sb{s}", [128, E * 512], F32)
        x8_sb = esb(f"x8_sb{s}", [128, E * 512], BF16)
        m1_sb = esb(f"m1_sb{s}", [128, 512], F32)
        m2_sb = esb(f"m2_sb{s}", [128, 512], F32)
        mt_sb = esb(f"mt_sb{s}", [128, 512], F32)
        z_sb = esb(f"z_sb{s}", [128, 512], F32)
        msk_sb = esb(f"msk_sb{s}", [128, 512], BF16)

        bar_e = emit_barrier()

        @blk.scalar
        def _(sc, s=s, bar_e=bar_e):
            sc = CE(sc, schain)
            sc.wait_ge(BAR.h, bar_e)
            BST.wait_slots(sc, (s + 1) * TC)
            for hh in range(HC):
                BFL.inc(sc.dma_start(out=b_fm[:, hh * 2048:(hh + 1) * 2048],
                                     in_=b_tm[s, :, hh * 128:(hh + 1) * 128],
                                     transpose=True), 16)

        @blk.gpsimd
        def _(g, s=s, bar_e=bar_e):
            g.wait_ge(BAR.h, bar_e)
            for oc in range(HC):
                for e in range(E):
                    gidx = s * HC * E + oc * E + e
                    pidx = gidx - NRING_E
                    if pidx >= 0:
                        s_p = pidx // (HC * E)
                        oc_p = (pidx % (HC * E)) // E
                        e_p = pidx % E
                        g.wait_ge(LMM.h, ((s_p * HC + oc_p) * NT5 + 3) * E + e_p + 1)
                    RWL.inc(g.dma_start(out=wr_ring[gidx % NRING_E][:],
                                        in_=wrt_h[e, oc, :, :, :]), gidx)

        @blk.tensor
        def _(te, s=s, bar_e=bar_e):
            te.wait_ge(BAR.h, bar_e)
            te.wait_ge(BFL.h, (s + 1) * HC * 16)
            pend = []
            for oc in range(HC):
                for t5 in range(NT5):
                    slot = (s * HC + oc) * NT5 + t5
                    for e in range(E):
                        idx = (s * HC + oc) * E + e
                        RWL.wait_slots(te, idx + 1)
                        gidx = slot * E + e
                        if gidx - s * NSLOT1 * E >= 2:
                            te.wait_ge(LEVA.h, gidx - 1)
                        wr_t = wr_ring[idx % NRING_E]
                        for hh in range(HC):
                            ins = te.matmul(ps_L[e % 2][:], wr_t[:, hh * 128:(hh + 1) * 128],
                                            b_fm[:, hh * 2048 + t5 * 512:hh * 2048 + (t5 + 1) * 512],
                                            start=(hh == 0), stop=(hh == HC - 1))
                            if hh == HC - 1:
                                LMM.inc(ins, 1)
                    pend.append(slot)
                    if len(pend) > DELAY:
                        emit_cmm(te, pend.pop(0))
            for fl in pend:
                emit_cmm(te, fl)

        @blk.scalar
        def _(sc, s=s):
            sc = CE(sc, schain)
            for oc in range(HC):
                for t5 in range(NT5):
                    slot = (s * HC + oc) * NT5 + t5
                    par = slot % 2
                    if slot - s * NSLOT1 >= 2:
                        sc.wait_ge(CTR.h, slot - 1)
                    for e in range(E):
                        sc.wait_ge(LMM.h, slot * E + e + 1)
                        sc.activation(L_sb[par][:, e * 512:(e + 1) * 512], ps_L[e % 2][:],
                                               AF.Identity,
                                               bias=g_rb[:, e * 16 + oc:e * 16 + oc + 1], inc=LEVA, inc_k=1)
                    sc.wait_ge(EV1.h, slot + 1)
                    sc.activation(z_sb[:], mt_sb[:], AF.Exp)
                    sc.activation(mt_sb[:], z_sb[:], AF.Ln, bias=1.0, inc=EA1, inc_k=1)
                    sc.wait_ge(EV2.h, slot + 1)
                    for e in range(E):
                        sc.activation(x8_sb[:, e * 512:(e + 1) * 512],
                                      d8_sb[:, e * 512:(e + 1) * 512], AF.Exp,
                                      inc=EA2 if e == E - 1 else None)

        @blk.vector
        def _(ve, s=s):
            ve = CE(ve, vchain)
            for oc in range(HC):
                for t5 in range(NT5):
                    slot = (s * HC + oc) * NT5 + t5
                    par = slot % 2
                    L = L_sb[par]
                    ve.wait_ge(LEVA.h, (slot + 1) * E)
                    ve.tensor_tensor(m1_sb[:], L[:, 0:512], L[:, 512:1024], ALU.max)
                    ve.tensor_tensor(m2_sb[:], L[:, 0:512], L[:, 512:1024], ALU.min)
                    for e in range(2, E):
                        Le = L[:, e * 512:(e + 1) * 512]
                        ve.tensor_tensor(mt_sb[:], m1_sb[:], Le, ALU.min)
                        ve.tensor_tensor(m2_sb[:], m2_sb[:], mt_sb[:], ALU.max)
                        ve.tensor_tensor(m1_sb[:], m1_sb[:], Le, ALU.max)
                    ve.tensor_tensor(mt_sb[:], m2_sb[:], m1_sb[:], ALU.subtract, inc=EV1, inc_k=1)
                    ve.wait_ge(EA1.h, slot + 1)
                    ve.tensor_tensor(z_sb[:], m1_sb[:], mt_sb[:], ALU.add)
                    for e in range(E):
                        ve.tensor_tensor(d8_sb[:, e * 512:(e + 1) * 512],
                                         L[:, e * 512:(e + 1) * 512], z_sb[:], ALU.subtract,
                                         inc=EV2 if e == E - 1 else None)
                    ve.wait_ge(EA2.h, slot + 1)
                    if slot - s * NSLOT1 >= 2:
                        ve.wait_ge(CMM.h, slot - 1)
                    ctr = ctr_sb[par]
                    for e in range(E):
                        ve.tensor_tensor(msk_sb[:], L[:, e * 512:(e + 1) * 512], m2_sb[:],
                                         ALU.is_ge)
                        ve.tensor_tensor(ctr[:, e * 512:(e + 1) * 512],
                                         x8_sb[:, e * 512:(e + 1) * 512], msk_sb[:], ALU.mult,
                                         inc=CTR if e == E - 1 else None)

        @blk.vector
        def _(ve, s=s):
            ve = CE(ve, vchain)
            ve.wait_ge(CMM.h, (s + 1) * NSLOT1)
            ve.tensor_scalar(c_sb[:], ps_c[:, 0:128], 1.0, None, ALU.mult, ALU.bypass, inc=CEV, inc_k=1)

        if c_dbg is not None:
            @blk.sync
            def _(sy, s=s):
                sy.wait_ge(CEV.h, s + 1)
                DBGS.inc(sy.dma_start(out=c_dbg[s, :, :], in_=c_sb[:]), 16)

        if stop == "E":
            @blk.sync
            def _(sy, s=s):
                sy.wait_ge(CEV.h, s + 1)
                if DBGS.n:
                    sy.wait_ge(DBGS.h, DBGS.n)
            ees.close()
            fes.close()
            ES.close()
            return nc

        bar_f = emit_barrier()
        ees.close()

        ffs = ExitStack()

        def xsb(name, shape, dt):
            return ffs.enter_context(nc.sbuf_tensor(name, list(shape), dt))

        we_ring = [xsb(f"we_ring{s}_{i}", [128, HC * 512], BF16) for i in range(3)]
        acc_sb = [xsb(f"acc{s}_{i}", [128, 512], F32) for i in range(TC)]
        fscr_sb = xsb(f"fscr_sb{s}", [128, 512], BF16)
        f_eb = xsb(f"f_eb{s}", [8, O], F32)
        f_eye = xsb(f"f_eye{s}", [128, 128], F32)

        @blk.sync
        def _(sy, s=s, bar_f=bar_f):
            sy.wait_ge(BAR.h, bar_f)
            EBL.inc(sy.dma_start(out=f_eb[:], in_=eb_h[:, :]), 16)
            EBL.inc(sy.dma_start(out=f_eye[:], in_=eye_h[:, :]), 16)

        @blk.tensor
        def _(te, s=s, bar_f=bar_f):
            te.wait_ge(BAR.h, bar_f)
            te.wait_ge(CEV.h, s + 1)
            te.wait_ge(EBL.h, (s + 1) * 32)
            for tc in range(TC):
                if tc >= 1:
                    te.wait_ge(CTEV.h, s * TC + tc)
                TPD.inc(te.transpose(ps_B[0][0:8, 0:128], c_sb[:, tc * E:(tc + 1) * E],
                                     f_eye[:]), 1)

        @blk.vector
        def _(ve, s=s):
            ve = CE(ve, vchain)
            for tc in range(TC):
                ve.wait_ge(TPD.h, s * TC + tc + 1)
                ve.tensor_scalar(cT_sb[tc][:], ps_B[0][0:8, 0:128], 1.0, None,
                                          ALU.mult, ALU.bypass, inc=CTEV, inc_k=1)

        @blk.gpsimd
        def _(g, s=s, bar_f=bar_f):
            g.wait_ge(BAR.h, bar_f)
            for oc4 in range(4):
                for e in range(E):
                    gidx = s * 4 * E + oc4 * E + e
                    if gidx >= 3:
                        g.wait_ge(YMM.h, (gidx - 3 + 1) * TC)
                    WEL.inc(g.dma_start(out=we_ring[gidx % 3][:], in_=we_h[e, oc4, :, :, :]), gidx)

        @blk.tensor
        def _(te, s=s):
            te.wait_ge(CTEV.h, (s + 1) * TC)
            for oc4 in range(4):
                for tc in range(TC):
                    jt = (s * 4 + oc4) * TC + tc
                    if jt - s * 4 * TC >= 2:
                        te.wait_ge(BEVA.h, jt - 1)
                    BMM.inc(te.matmul(ps_B[jt % 2][:], cT_sb[tc][:],
                                      f_eb[0:8, oc4 * 512:(oc4 + 1) * 512],
                                      start=True, stop=True), 1)
                for e in range(E):
                    idx = (s * 4 + oc4) * E + e
                    WEL.wait_slots(te, idx + 1)
                    wet = we_ring[idx % 3]
                    for tc in range(TC):
                        jt = idx * TC + tc
                        if jt - s * 4 * E * TC >= 2:
                            te.wait_ge(YEV.h, jt - 1)
                        for hh in range(HC):
                            ins = te.matmul(ps_Y[jt % 2][:],
                                            b_fm[:, hh * 2048 + tc * 128:hh * 2048 + (tc + 1) * 128],
                                            wet[:, hh * 512:(hh + 1) * 512],
                                            start=(hh == 0), stop=(hh == HC - 1))
                            if hh == HC - 1:
                                YMM.inc(ins, 1)

        @blk.vector
        def _(ve, s=s):
            ve = CE(ve, vchain)
            for oc4 in range(4):
                for tc in range(TC):
                    jt = (s * 4 + oc4) * TC + tc
                    ve.wait_ge(BMM.h, jt + 1)
                    if oc4 >= 1:
                        ve.wait_ge(FSQ.h, jt - TC + 1)
                        ve.wait_ge(FFST.h, (s * 4 + oc4) * TC * 16)
                    ve.tensor_scalar(acc_sb[tc][:], ps_B[jt % 2][:], 1.0, None,
                                              ALU.mult, ALU.bypass, inc=BEVA, inc_k=1)
                for e in range(E):
                    idx = (s * 4 + oc4) * E + e
                    for tc in range(TC):
                        jt = idx * TC + tc
                        ve.wait_ge(YMM.h, jt + 1)
                        ve.scalar_tensor_tensor(acc_sb[tc][:], ps_Y[jt % 2][:],
                                                        c_sb[:, tc * E + e:tc * E + e + 1],
                                                        acc_sb[tc][:], ALU.mult, ALU.add, inc=YEV, inc_k=1)
                for tc in range(TC):
                    col = tc * 4 + oc4
                    ve.tensor_scalar(fscr_sb[:], acc_sb[tc][:], 1.0, None, ALU.mult,
                                             ALU.add, accum_out=fsum_sb[:, col:col + 1], inc=FST, inc_k=1)

        @blk.scalar
        def _(sc, s=s):
            sc = CE(sc, schain)
            for oc4 in range(4):
                for tc in range(TC):
                    jt = (s * 4 + oc4) * TC + tc
                    col = tc * 4 + oc4
                    sc.wait_ge(FST.h, jt + 1)
                    sc.activation(fscr_sb[:], acc_sb[tc][:], AF.Square,
                                          accum_out=fsq_sb[:, col:col + 1], inc=FSQ, inc_k=1)

        @blk.sync
        def _(sy, s=s):
            for oc4 in range(4):
                for tc in range(TC):
                    jt = (s * 4 + oc4) * TC + tc
                    sy.wait_ge(FSQ.h, jt + 1)
                    FFST.inc(sy.dma_start(out=ffn_sc[s, tc, :, oc4 * 512:(oc4 + 1) * 512],
                                          in_=acc_sb[tc][:]), 16)

        if stop == "F":
            @blk.sync
            def _(sy, s=s):
                sy.wait_ge(FFST.h, FFST.n)
                if DBGS.n:
                    sy.wait_ge(DBGS.h, DBGS.n)
            ffs.close()
            fes.close()
            ES.close()
            return nc

        bar_g = emit_barrier()
        ffs.close()

        ggs = ExitStack()

        def gsb(name, shape, dt):
            return ggs.enter_context(nc.sbuf_tensor(name, list(shape), dt))

        pg_x1 = [gsb(f"pg_x1{s}_{i}", [128, H], F32) for i in range(2)]
        pg_f = [gsb(f"pg_f{s}_{i}", [128, H], F32) for i in range(2)]
        pg_o = [gsb(f"pg_o{s}_{i}", [128, H], F32) for i in range(2)]
        pg_st = gsb(f"pg_st{s}", [128, 8], F32)

        @blk.sync
        def _(sy, s=s, bar_g=bar_g):
            sy.wait_ge(BAR.h, bar_g)
            for tc in range(TC):
                it = s * TC + tc
                if tc >= 2:
                    sy.wait_ge(GV1.h, 2 * (it - 2) + 2)
                GXL.inc(sy.dma_start(out=pg_x1[it % 2][:],
                                     in_=x1_sc[s, tc * 128:(tc + 1) * 128, :]), it)
                GXL.inc(sy.dma_start(out=pg_f[it % 2][:], in_=ffn_sc[s, tc, :, :]), it)
                if tc >= 1:
                    jt = it - 1
                    sy.wait_ge(GV1.h, 2 * jt + 2)
                    OUTS.inc(sy.dma_start(
                        out=out_hbm[s, (jt % TC) * 128:((jt % TC) + 1) * 128, :],
                        in_=pg_o[jt % 2][:]), jt)
            jt = (s + 1) * TC - 1
            sy.wait_ge(GV1.h, 2 * jt + 2)
            OUTS.inc(sy.dma_start(
                out=out_hbm[s, (jt % TC) * 128:((jt % TC) + 1) * 128, :],
                in_=pg_o[jt % 2][:]), jt)

        @blk.vector
        def _(ve, s=s):
            ve = CE(ve, vchain)
            ve.wait_ge(FSQ.h, (s + 1) * 4 * TC)
            for tc in range(TC):
                it = s * TC + tc
                GXL.wait_slots(ve, it + 1)
                ve.tensor_tensor(pg_st[:, 0:1], fsum_sb[:, tc * 4:tc * 4 + 1],
                                 fsum_sb[:, tc * 4 + 1:tc * 4 + 2], ALU.add)
                ve.tensor_tensor(pg_st[:, 1:2], fsum_sb[:, tc * 4 + 2:tc * 4 + 3],
                                 fsum_sb[:, tc * 4 + 3:tc * 4 + 4], ALU.add)
                ve.tensor_tensor(pg_st[:, 0:1], pg_st[:, 0:1], pg_st[:, 1:2], ALU.add)
                ve.tensor_tensor(pg_st[:, 2:3], fsq_sb[:, tc * 4:tc * 4 + 1],
                                 fsq_sb[:, tc * 4 + 1:tc * 4 + 2], ALU.add)
                ve.tensor_tensor(pg_st[:, 3:4], fsq_sb[:, tc * 4 + 2:tc * 4 + 3],
                                 fsq_sb[:, tc * 4 + 3:tc * 4 + 4], ALU.add)
                ve.tensor_tensor(pg_st[:, 2:3], pg_st[:, 2:3], pg_st[:, 3:4], ALU.add)
                ve.tensor_scalar(pg_st[:, 0:1], pg_st[:, 0:1], 1.0 / O, None, ALU.mult, ALU.bypass)
                ve.tensor_scalar(pg_st[:, 2:3], pg_st[:, 2:3], 1.0 / O, None, ALU.mult, ALU.bypass)
                ve.tensor_tensor(pg_st[:, 1:2], pg_st[:, 0:1], pg_st[:, 0:1], ALU.mult)
                ve.tensor_tensor(pg_st[:, 1:2], pg_st[:, 2:3], pg_st[:, 1:2],
                                         ALU.subtract, inc=GV1, inc_k=1)
                ve.wait_ge(GS1.h, it + 1)
                ve.scalar_tensor_tensor(pg_st[:, 4:5], pg_st[:, 0:1], -1.0, pg_st[:, 5:6],
                                        ALU.mult, ALU.mult)
                ot = pg_o[it % 2]
                if tc >= 2:
                    OUTS.wait_slots(ve, it - 1)
                ve.tensor_scalar(ot[:], pg_x1[it % 2][:], pg_st[:, 4:5], None, ALU.add, ALU.bypass)
                ve.scalar_tensor_tensor(ot[:], pg_f[it % 2][:], pg_st[:, 5:6], ot[:],
                                                ALU.mult, ALU.add, inc=GV1, inc_k=1)

        @blk.scalar
        def _(sc, s=s):
            sc = CE(sc, schain)
            for tc in range(TC):
                it = s * TC + tc
                sc.wait_ge(GV1.h, 2 * it + 1)
                sc.activation(pg_st[:, 6:7], pg_st[:, 1:2], AF.Ln, bias=g_eps[:, 0:1])
                sc.activation(pg_st[:, 5:6], pg_st[:, 6:7], AF.Exp, scale=-0.5, inc=GS1, inc_k=1)

        bar_end = emit_barrier()
        ggs.close()

    @blk.sync
    def _(sy):
        OUTS.wait_all(sy)
        if DBGS.n:
            sy.wait_ge(DBGS.h, DBGS.n)

    fes.close()
    ES.close()
    return nc


def _pack_weights(L0_attn_time_decay, L0_attn_time_first, L0_attn_time_mix_key,
                  L0_attn_time_mix_value, L0_attn_time_mix_receptance, L0_attn_W_key,
                  L0_attn_W_value, L0_attn_W_receptance, L0_attn_W_output,
                  L0_ffn_router_w, L0_ffn_router_b, L0_ffn_expert_w, L0_ffn_expert_b):
    C = np.ascontiguousarray

    def kv_pack(w):
        return C(np.asarray(w, np.float32).T.reshape(HC, 128, HC, 128).transpose(2, 1, 0, 3))

    def vec_pack(v):
        return C(np.asarray(v, np.float32).reshape(-1).reshape(HC, 128).T)

    wrt = np.asarray(L0_ffn_router_w, np.float32).reshape(O, E, H).transpose(1, 2, 0)
    wrt = C(wrt.reshape(E, HC, 128, HC, 128).transpose(0, 3, 2, 1, 4))
    rb = np.asarray(L0_ffn_router_b, np.float32).reshape(O, E).T
    rb = C(rb.reshape(E, 16, 128).transpose(2, 0, 1).reshape(128, 128))
    we = np.asarray(L0_ffn_expert_w, np.float32).transpose(0, 2, 1)
    we = C(we.reshape(E, HC, 128, 4, 512).transpose(0, 3, 2, 1, 4))
    return {
        "wk": kv_pack(L0_attn_W_key),
        "wv": kv_pack(L0_attn_W_value),
        "wr": kv_pack(L0_attn_W_receptance),
        "wo": C(np.asarray(L0_attn_W_output, np.float32).T),
        "wrt": wrt,
        "rb": rb,
        "we": we,
        "eb": C(np.asarray(L0_ffn_expert_b, np.float32)),
        "td": vec_pack(L0_attn_time_decay),
        "tf": vec_pack(L0_attn_time_first),
        "tmk": vec_pack(L0_attn_time_mix_key),
        "tmv": vec_pack(L0_attn_time_mix_value),
        "tmr": vec_pack(L0_attn_time_mix_receptance),
        "eye": np.eye(128, dtype=np.float32),
    }


def kernel(**inputs):
    x = np.asarray(inputs["L0_attn_hidden"], np.float32)
    shared = _pack_weights(**{k: v for k, v in inputs.items() if k != "L0_attn_hidden"})
    nc = build(dbg=DBG)
    in_maps = []
    for c in range(NCORES):
        m = dict(shared)
        m["x"] = np.ascontiguousarray(x[c * S:(c + 1) * S])
        in_maps.append(m)
    res = run_bass_kernel_spmd(nc, in_maps, core_ids=list(range(NCORES)))
    out = np.concatenate([np.asarray(r["out"]) for r in res.results], axis=0)
    return out


# version 04 (final)
# speedup vs baseline: 1.0340x; 1.0340x over previous
import numpy as np
from contextlib import ExitStack

import concourse.bass as bass
import concourse.mybir as mybir
from concourse.bass_utils import run_bass_kernel_spmd

F32 = mybir.dt.float32
BF16 = mybir.dt.bfloat16
AF = mybir.ActivationFunctionType
ALU = mybir.AluOpType

NCORES = 8
S = 2
T = 2048
H = 2048
HC = H // 128
TC = T // 128
NT5 = T // 512
E = 8
O = 2048
EPS = 1e-5
NRING = 10

DBG = ()


class Sem:
    __slots__ = ("h", "n")

    def __init__(self, es, nc, name):
        self.h = es.enter_context(nc.semaphore(name))
        self.n = 0

    def inc(self, instr, k=1):
        instr.then_inc(self.h, k)
        self.n += k
        return instr


class DSem:

    def __init__(self, es, nc, name, nway=2, m=1):
        self.s = [Sem(es, nc, f"{name}_{i}") for i in range(nway)]
        self.nway = nway
        self.m = m
        self.slots = 0

    def inc(self, instr, slot):
        self.s[slot % self.nway].inc(instr, 16)

    def wait_slots(self, eng, n):
        for j in range(self.nway):
            cnt = (n - j + self.nway - 1) // self.nway
            if cnt > 0:
                eng.wait_ge(self.s[j].h, cnt * self.m * 16)

    def wait_all(self, eng):
        for j in range(self.nway):
            if self.s[j].n:
                eng.wait_ge(self.s[j].h, self.s[j].n)


class EngChain:
    def __init__(self, es, nc, name):
        self.sem = Sem(es, nc, name)
        self.last = None


class CE:

    def __init__(self, eng, chain):
        self._e = eng
        self._c = chain

    def wait_ge(self, h, v):
        self._e.wait_ge(h, v)

    def dma_start(self, **kw):
        return self._e.dma_start(**kw)

    def _pre(self):
        if self._c.last is not None:
            self._e.wait_ge(self._c.last[0], self._c.last[1])

    def _post(self, ins, inc, inc_k):
        if inc is not None:
            inc.inc(ins, inc_k)
            self._c.last = (inc.h, inc.n)
        else:
            s = self._c.sem
            s.inc(ins, 1)
            self._c.last = (s.h, s.n)
        return ins

    def tensor_tensor(self, out, a, b, op, inc=None, inc_k=1):
        self._pre()
        return self._post(self._e.tensor_tensor(out, a, b, op), inc, inc_k)

    def tensor_scalar(self, out, a, s1, s2, op0, op1, accum_out=None, inc=None, inc_k=1):
        self._pre()
        return self._post(self._e.tensor_scalar(out, a, s1, s2, op0, op1,
                                                accum_out=accum_out), inc, inc_k)

    def scalar_tensor_tensor(self, out, in0, scalar, in1, op0, op1, inc=None, inc_k=1):
        self._pre()
        return self._post(self._e.scalar_tensor_tensor(out, in0, scalar, in1, op0, op1),
                          inc, inc_k)

    def tensor_tensor_scan(self, out, d0, d1, init, op0, op1, inc=None, inc_k=1):
        self._pre()
        return self._post(self._e.tensor_tensor_scan(out, d0, d1, init, op0, op1),
                          inc, inc_k)

    def activation(self, out, in_, func, bias=None, scale=None, accum_out=None,
                   inc=None, inc_k=1):
        kw = {}
        if bias is not None:
            kw["bias"] = bias
        if scale is not None:
            kw["scale"] = scale
        if accum_out is not None:
            kw["accum_out"] = accum_out
        self._pre()
        return self._post(self._e.activation(out, in_, func, **kw), inc, inc_k)

    def nop(self, inc=None, inc_k=1):
        self._pre()
        return self._post(self._e.nop(), inc, inc_k)


def build(dbg=(), stop=None):
    nc = bass.Bass("TRN2", target_bir_lowering=False, debug=False)
    ES = ExitStack()

    def sbt(name, shape, dt):
        return ES.enter_context(nc.sbuf_tensor(name, list(shape), dt))

    def sem(name):
        return Sem(ES, nc, name)

    def din(name, shape, dt=F32):
        return nc.dram_tensor(name, list(shape), dt, kind="ExternalInput").ap()

    def dscr(name, shape, dt):
        kind = "ExternalOutput" if name in dbg else "Internal"
        return nc.dram_tensor(name, list(shape), dt, kind=kind).ap()

    x_hbm = din("x", [S, T, H])
    wk_h = din("wk", [HC, 128, HC, 128])
    wv_h = din("wv", [HC, 128, HC, 128])
    wr_h = din("wr", [HC, 128, HC, 128])
    wo_h = din("wo", [H, H])
    wrt_h = din("wrt", [E, HC, 128, HC, 128])
    rb_h = din("rb", [128, 128])
    we_h = din("we", [E, 4, 128, HC, 512])
    eb_h = din("eb", [8, O])
    td_h = din("td", [128, HC])
    tf_h = din("tf", [128, HC])
    tmk_h = din("tmk", [128, HC])
    tmv_h = din("tmv", [128, HC])
    tmr_h = din("tmr", [128, HC])
    eye_h = din("eye", [128, 128])

    out_hbm = nc.dram_tensor("out", [S, T, H], F32, kind="ExternalOutput").ap()

    a_tm = dscr("a_tm", [S, T, H], BF16)
    u_sc = dscr("u_sc", [S, HC, 128, T], BF16)
    x1_sc = dscr("x1_sc", [S, T, H], F32)
    b_tm = dscr("b_tm", [S, T, H], BF16)
    ffn_sc = dscr("ffn_sc", [S, TC, 128, O], F32)
    c_dbg = dscr("c_dbg", [S, 128, 128], F32) if "c_dbg" in dbg else None
    kin_dbg = dscr("kin_dbg", [NT5, 128, HC * 512], BF16) if "kin_dbg" in dbg else None
    kpr_dbg = dscr("kpr_dbg", [NT5, 2, 128, 512], F32) if "kin_dbg" in dbg else None

    g_td = sbt("g_td", [128, HC], F32)
    g_tf = sbt("g_tf", [128, HC], F32)
    g_tmk = sbt("g_tmk", [128, HC], F32)
    g_tmv = sbt("g_tmv", [128, HC], F32)
    g_tmr = sbt("g_tmr", [128, HC], F32)
    g_w = sbt("g_w", [128, HC], F32)
    g_etf = sbt("g_etf", [128, HC], F32)
    g_rb = sbt("g_rb", [128, 128], F32)
    g_eb = sbt("g_eb", [8, O], F32)
    g_eye = sbt("g_eye", [128, 128], F32)
    g_ones512 = sbt("g_ones512", [128, 512], F32)
    g_onescol = sbt("g_onescol", [128, 1], BF16)
    g_zerocol = sbt("g_zerocol", [128, 1], BF16)
    g_eps = sbt("g_eps", [128, 1], F32)

    LD = sem("LD")
    GM = sem("GM")
    CA = sem("CA")
    CV = sem("CV")
    BAR = sem("BAR")

    vchain = EngChain(ES, nc, "VSELF")
    schain = EngChain(ES, nc, "ASELF")
    blk = ES.enter_context(nc.Block())

    @blk.sync
    def _(sy):
        for dst, src in [(g_td, td_h), (g_tf, tf_h), (g_tmk, tmk_h), (g_tmv, tmv_h),
                         (g_tmr, tmr_h), (g_rb, rb_h), (g_eb, eb_h), (g_eye, eye_h)]:
            LD.inc(sy.dma_start(out=dst[:], in_=src[:, :]), 16)
    ld_const = LD.n

    @blk.gpsimd
    def _(g):
        g.memset(g_ones512[:], 1.0)
        g.memset(g_onescol[:], 1.0)
        g.memset(g_zerocol[:], 0.0)
        GM.inc(g.memset(g_eps[:], EPS), 1)

    @blk.scalar
    def _(sc):
        sc = CE(sc, schain)
        sc.wait_ge(LD.h, ld_const)
        sc.activation(g_w[:], g_td[:], AF.Exp)
        sc.activation(g_etf[:], g_tf[:], AF.Exp, inc=CA, inc_k=1)

    @blk.vector
    def _(ve):
        ve = CE(ve, vchain)
        ve.wait_ge(CA.h, 1)
        ve.wait_ge(GM.h, 1)
        ve.tensor_scalar(g_w[:], g_w[:], -1.0, None, ALU.mult, ALU.bypass, inc=CV, inc_k=1)

    aes = ExitStack()

    def asb(name, shape, dt):
        return aes.enter_context(nc.sbuf_tensor(name, list(shape), dt))

    def apsum(name):
        return aes.enter_context(nc.psum_tensor(name, [128, 512], F32))

    pa_x = [asb(f"pa_x{i}", [128, H], F32) for i in range(2)]
    pa_scr = asb("pa_scr", [128, H], BF16)
    pa_st = asb("pa_st", [128, 8], F32)
    pa_a = [asb(f"pa_a{i}", [128, H], BF16) for i in range(2)]

    pb_af = [asb(f"pb_af{i}", [128, HC * 512], BF16) for i in range(2)]
    pb_kin = asb("pb_kin", [128, HC * 512], BF16)
    pb_vin = asb("pb_vin", [128, HC * 512], BF16)
    pb_rin = asb("pb_rin", [128, HC * 512], BF16)
    pb_d = asb("pb_d", [128, 512], F32)
    pb_wt = [asb(f"pb_wt{i}", [128, 3 * H], BF16) for i in range(2)]
    pb_k = [asb(f"pb_k{i}", [128, 512], F32) for i in range(2)]
    pb_v = [asb(f"pb_v{i}", [128, 512], F32) for i in range(2)]
    pb_E = [asb(f"pb_E{i}", [128, 512], F32) for i in range(2)]
    pb_wbuf = asb("pb_wbuf", [128, 512], F32)
    pb_mx = asb("pb_mx", [128, 512], F32)
    pb_d1 = asb("pb_d1", [128, 512], F32)
    pb_d2 = asb("pb_d2", [128, 512], F32)
    pb_e1s = asb("pb_e1s", [128, 512], F32)
    pb_alpha = asb("pb_alpha", [128, 512], F32)
    pb_e2s = asb("pb_e2s", [128, 512], F32)
    pb_e2v = asb("pb_e2v", [128, 512], F32)
    pb_num = asb("pb_num", [128, 512], F32)
    pb_den = asb("pb_den", [128, 512], F32)
    pb_t1 = asb("pb_t1", [128, 512], F32)
    pb_NUM = asb("pb_NUM", [128, 512], F32)
    pb_t3 = asb("pb_t3", [128, 512], F32)
    pb_DEN = asb("pb_DEN", [128, 512], F32)
    pb_D2 = asb("pb_D2", [128, 512], F32)
    pb_ln = asb("pb_ln", [128, 512], F32)
    pb_r2 = asb("pb_r2", [128, 512], F32)
    pb_u = [asb(f"pb_u{i}", [128, 512], BF16) for i in range(2)]
    pb_mxs = [asb(f"pb_mxs{i}", [128, 1], F32) for i in range(HC)]
    pb_ns = [asb(f"pb_ns{i}", [128, 1], F32) for i in range(HC)]
    pb_ds = [asb(f"pb_ds{i}", [128, 1], F32) for i in range(HC)]

    ps_k = [apsum(f"ps_k{i}") for i in range(2)]
    ps_v = [apsum(f"ps_v{i}") for i in range(2)]
    ps_r = [apsum(f"ps_r{i}") for i in range(2)]

    AXL = DSem(ES, nc, "AXL")
    AS1 = sem("AS1"); AV1 = sem("AV1"); AS2 = sem("AS2")
    AV2 = sem("AV2"); AAP = sem("AAP")
    AST = DSem(ES, nc, "AST")
    TLD = DSem(ES, nc, "TLD", m=HC)
    MIXD = sem("MIXD")
    WLD = DSem(ES, nc, "WLD", m=3)
    KMM = sem("KMM"); VMM = sem("VMM"); RMM = sem("RMM")
    KEV = sem("KEV"); VEV = sem("VEV"); REV = sem("REV")
    SDA = sem("SDA"); SAD = sem("SAD"); UDONE = sem("UDONE")
    UST = DSem(ES, nc, "UST")
    WOLD = sem("WOLD")
    ULD = DSem(ES, nc, "ULD")
    CXL = DSem(ES, nc, "CXL")
    OMM = sem("OMM"); OEV = sem("OEV"); OSQ = sem("OSQ")
    CV1 = sem("CV1"); CS1 = sem("CS1"); CV2 = sem("CV2"); CS2 = sem("CS2")
    CV3 = sem("CV3"); CS3 = sem("CS3"); CV4 = sem("CV4"); CAPP = sem("CAPP")
    DBGS = sem("DBGS")
    X1ST = DSem(ES, nc, "X1ST")
    BST = DSem(ES, nc, "BST")

    for s in range(S):
        @blk.sync
        def _(sy, s=s):
            for tc in range(TC):
                it = s * TC + tc
                if it >= 2:
                    sy.wait_ge(AAP.h, it - 1)
                AXL.inc(sy.dma_start(out=pa_x[it % 2][:],
                                     in_=x_hbm[s, tc * 128:(tc + 1) * 128, :]), it)
                if tc >= 1:
                    jt = it - 1
                    sy.wait_ge(AAP.h, jt + 1)
                    AST.inc(sy.dma_start(out=a_tm[s, (jt % TC) * 128:((jt % TC) + 1) * 128, :],
                                         in_=pa_a[jt % 2][:]), jt)
            jt = s * TC + TC - 1
            sy.wait_ge(AAP.h, jt + 1)
            AST.inc(sy.dma_start(out=a_tm[s, (jt % TC) * 128:((jt % TC) + 1) * 128, :],
                                 in_=pa_a[jt % 2][:]), jt)

        @blk.scalar
        def _(sc, s=s):
            sc = CE(sc, schain)
            for tc in range(TC):
                it = s * TC + tc
                xb = pa_x[it % 2]
                AXL.wait_slots(sc, it + 1)
                sc.activation(pa_scr[:], xb[:], AF.Identity, accum_out=pa_st[:, 0:1])
                sc.activation(pa_scr[:], xb[:], AF.Square, accum_out=pa_st[:, 1:2], inc=AS1, inc_k=1)
                sc.wait_ge(AV1.h, it + 1)
                sc.activation(pa_st[:, 4:5], pa_st[:, 3:4], AF.Ln, bias=g_eps[:, 0:1])
                sc.activation(pa_st[:, 5:6], pa_st[:, 4:5], AF.Exp, scale=-0.5, inc=AS2, inc_k=1)
                sc.wait_ge(AV2.h, it + 1)
                if it >= 2:
                    AST.wait_slots(sc, it - 1)
                sc.activation(pa_a[it % 2][:], xb[:], AF.Identity,
                                      bias=pa_st[:, 6:7], scale=pa_st[:, 5:6], inc=AAP, inc_k=1)

        @blk.vector
        def _(ve, s=s):
            ve = CE(ve, vchain)
            for tc in range(TC):
                it = s * TC + tc
                ve.wait_ge(AS1.h, it + 1)
                ve.tensor_scalar(pa_st[:, 2:3], pa_st[:, 0:1], 1.0 / H, None, ALU.mult, ALU.bypass)
                ve.tensor_scalar(pa_st[:, 7:8], pa_st[:, 1:2], 1.0 / H, None, ALU.mult, ALU.bypass)
                ve.tensor_tensor(pa_st[:, 3:4], pa_st[:, 2:3], pa_st[:, 2:3], ALU.mult)
                ve.tensor_tensor(pa_st[:, 3:4], pa_st[:, 7:8], pa_st[:, 3:4], ALU.subtract, inc=AV1, inc_k=1)
                ve.wait_ge(AS2.h, it + 1)
                ve.scalar_tensor_tensor(pa_st[:, 6:7], pa_st[:, 2:3], -1.0,
                                                pa_st[:, 5:6], ALU.mult, ALU.mult, inc=AV2, inc_k=1)


    if stop == "A":
        @blk.sync
        def _(sy):
            AST.wait_all(sy)
        aes.close()
        ES.close()
        return nc

    for s in range(S):
        @blk.sync
        def _(sy, s=s):
            for t5 in range(NT5):
                it = s * NT5 + t5
                AST.wait_slots(sy, s * TC + (t5 + 1) * 4)
                if it >= 2:
                    sy.wait_ge(MIXD.h, it)
                for hh in range(HC):
                    TLD.inc(sy.dma_start(
                        out=pb_af[it % 2][:, hh * 512:(hh + 1) * 512],
                        in_=a_tm[s, t5 * 512:(t5 + 1) * 512, hh * 128:(hh + 1) * 128],
                        transpose=True), it)

        @blk.gpsimd
        def _(g, s=s):
            def ustore(g, jt):
                t5j = (jt // HC) % NT5
                acj = jt % HC
                g.wait_ge(UDONE.h, jt + 1)
                UST.inc(g.dma_start(out=u_sc[s, acj, :, t5j * 512:(t5j + 1) * 512],
                                    in_=pb_u[jt % 2][:]), jt)
            for t5 in range(NT5):
                for ac in range(HC):
                    lc = t5 * HC + ac
                    it = s * NT5 * HC + lc
                    if it >= 2:
                        g.wait_ge(RMM.h, it - 1)
                    WLD.inc(g.dma_start(out=pb_wt[it % 2][:, 0:H], in_=wk_h[ac, :, :, :]), it)
                    WLD.inc(g.dma_start(out=pb_wt[it % 2][:, H:2 * H], in_=wv_h[ac, :, :, :]), it)
                    WLD.inc(g.dma_start(out=pb_wt[it % 2][:, 2 * H:3 * H], in_=wr_h[ac, :, :, :]), it)
                    if lc >= 1:
                        ustore(g, it - 1)
                    if kin_dbg is not None and s == 0 and ac == HC - 1:
                        g.wait_ge(MIXD.h, t5 + 1)
                        DBGS.inc(g.dma_start(out=kin_dbg[t5, :, :], in_=pb_kin[:]), 16)
                        itc0 = t5 * HC
                        g.wait_ge(KEV.h, itc0 + 1)
                        DBGS.inc(g.dma_start(out=kpr_dbg[t5, 0, :, :], in_=pb_k[itc0 % 2][:]), 16)
                        g.wait_ge(VEV.h, itc0 + 1)
                        DBGS.inc(g.dma_start(out=kpr_dbg[t5, 1, :, :], in_=pb_v[itc0 % 2][:]), 16)
            ustore(g, s * NT5 * HC + NT5 * HC - 1)

        @blk.vector
        def _(ve, s=s):
            ve = CE(ve, vchain)
            if s == 0:
                ve.wait_ge(GM.h, 1)
            for t5 in range(NT5):
                it = s * NT5 + t5
                TLD.wait_slots(ve, it + 1)
                if it >= 1:
                    ve.wait_ge(RMM.h, it * HC)
                cur = pb_af[it % 2]
                prv = pb_af[(it - 1) % 2]
                for hh in range(HC):
                    c0 = cur[:, hh * 512:hh * 512 + 1]
                    cm = cur[:, hh * 512 + 1:(hh + 1) * 512]
                    cm0 = cur[:, hh * 512:(hh + 1) * 512 - 1]
                    shift0 = g_zerocol[:, 0:1] if t5 == 0 else prv[:, hh * 512 + 511:hh * 512 + 512]
                    ve.tensor_tensor(pb_d[:, 1:512], cm, cm0, ALU.subtract)
                    ve.tensor_tensor(pb_d[:, 0:1], c0, shift0, ALU.subtract)
                    for mi, (mix, tmcol) in enumerate(((pb_kin, g_tmk), (pb_vin, g_tmv),
                                                       (pb_rin, g_tmr))):
                        om = mix[:, hh * 512:(hh + 1) * 512]
                        ve.scalar_tensor_tensor(om[:, 1:512], pb_d[:, 1:512],
                                                tmcol[:, hh:hh + 1], cm0, ALU.mult, ALU.add)
                        ve.scalar_tensor_tensor(om[:, 0:1], pb_d[:, 0:1],
                                                tmcol[:, hh:hh + 1], shift0, ALU.mult, ALU.add,
                                                inc=MIXD if (hh == HC - 1 and mi == 2) else None)
                for ac in range(HC):
                    itc = it * HC + ac
                    par = itc % 2
                    ve.wait_ge(KMM.h, itc + 1)
                    ve.tensor_scalar(pb_k[par][:], ps_k[par][:], 1.0, None,
                                             ALU.mult, ALU.bypass, inc=KEV, inc_k=1)
                    ve.wait_ge(VMM.h, itc + 1)
                    ve.tensor_scalar(pb_v[par][:], ps_v[par][:], 1.0, None,
                                             ALU.mult, ALU.bypass, inc=VEV, inc_k=1)
                    ve.tensor_scalar(pb_wbuf[:], g_ones512[:], g_w[:, ac:ac + 1], None,
                                     ALU.mult, ALU.bypass)
                    ve.tensor_tensor_scan(pb_mx[:], pb_wbuf[:], pb_k[par][:],
                                          pb_mxs[ac][:] if t5 > 0 else -1e38,
                                          ALU.add, ALU.max)
                    if t5 > 0:
                        ve.tensor_tensor(pb_d1[:, 0:1], pb_mxs[ac][:], pb_mx[:, 0:1],
                                         ALU.subtract)
                    else:
                        ve.tensor_scalar(pb_d1[:, 0:1], pb_mx[:, 0:1], -1.0, -1e38,
                                         ALU.mult, ALU.add)
                    ve.tensor_tensor(pb_d1[:, 1:512], pb_mx[:, 0:511], pb_mx[:, 1:512],
                                     ALU.subtract)
                    ve.tensor_tensor(pb_d2[:], pb_k[par][:], pb_mx[:], ALU.subtract, inc=SDA, inc_k=1)
                    ve.wait_ge(SAD.h, itc * 2 + 1)
                    ve.tensor_tensor(pb_e2v[:], pb_e2s[:], pb_v[par][:], ALU.mult)
                    ve.tensor_tensor_scan(pb_num[:], pb_e1s[:], pb_e2v[:],
                                          pb_ns[ac][:] if t5 > 0 else 0.0,
                                          ALU.mult, ALU.add)
                    ve.tensor_tensor_scan(pb_den[:], pb_e1s[:], pb_e2s[:],
                                          pb_ds[ac][:] if t5 > 0 else 0.0,
                                          ALU.mult, ALU.add)
                    ve.tensor_scalar(pb_e2v[:], pb_e2v[:], g_etf[:, ac:ac + 1], None,
                                     ALU.mult, ALU.bypass)
                    ve.tensor_scalar(pb_e2s[:], pb_e2s[:], g_etf[:, ac:ac + 1], None,
                                     ALU.mult, ALU.bypass)
                    if t5 > 0:
                        ve.tensor_tensor(pb_t1[:, 0:1], pb_alpha[:, 0:1], pb_ns[ac][:],
                                         ALU.mult)
                    else:
                        ve.tensor_scalar(pb_t1[:, 0:1], pb_alpha[:, 0:1], 0.0, None,
                                         ALU.mult, ALU.bypass)
                    ve.tensor_tensor(pb_t1[:, 1:512], pb_alpha[:, 1:512], pb_num[:, 0:511],
                                     ALU.mult)
                    ve.tensor_tensor(pb_NUM[:], pb_t1[:], pb_e2v[:], ALU.add)
                    if t5 > 0:
                        ve.tensor_tensor(pb_t3[:, 0:1], pb_alpha[:, 0:1], pb_ds[ac][:],
                                         ALU.mult)
                    else:
                        ve.tensor_scalar(pb_t3[:, 0:1], pb_alpha[:, 0:1], 0.0, None,
                                         ALU.mult, ALU.bypass)
                    ve.tensor_tensor(pb_t3[:, 1:512], pb_alpha[:, 1:512], pb_den[:, 0:511],
                                     ALU.mult)
                    ve.tensor_tensor(pb_DEN[:], pb_t3[:], pb_e2s[:], ALU.add)
                    ve.tensor_scalar(pb_mxs[ac][:], pb_mx[:, 511:512], 1.0, None,
                                     ALU.mult, ALU.bypass)
                    ve.tensor_scalar(pb_ns[ac][:], pb_num[:, 511:512], 1.0, None,
                                     ALU.mult, ALU.bypass)
                    ve.tensor_scalar(pb_ds[ac][:], pb_den[:, 511:512], 1.0, None,
                                     ALU.mult, ALU.bypass)
                    ve.wait_ge(REV.h, itc + 1)
                    ve.scalar_tensor_tensor(pb_D2[:], pb_E[par][:], 1.0, pb_DEN[:],
                                                    ALU.add, ALU.mult, inc=SDA, inc_k=1)
                    ve.wait_ge(SAD.h, itc * 2 + 2)
                    if itc >= 2:
                        UST.wait_slots(ve, itc - 1)
                    ve.tensor_tensor(pb_u[par][:], pb_NUM[:], pb_r2[:], ALU.mult, inc=UDONE, inc_k=1)

        @blk.tensor
        def _(te, s=s):
            for t5 in range(NT5):
                it5 = s * NT5 + t5
                te.wait_ge(MIXD.h, it5 + 1)
                for ac in range(HC):
                    itc = it5 * HC + ac
                    WLD.wait_slots(te, itc + 1)
                    wt = pb_wt[itc % 2]
                    if itc >= 2:
                        te.wait_ge(KEV.h, itc - 1)
                        te.wait_ge(VEV.h, itc - 1)
                        te.wait_ge(REV.h, itc - 1)
                    for off, mix, pse, msem in ((0, pb_kin, ps_k, KMM), (H, pb_vin, ps_v, VMM),
                                                (2 * H, pb_rin, ps_r, RMM)):
                        for hh in range(HC):
                            ins = te.matmul(pse[itc % 2][:],
                                            wt[:, off + hh * 128:off + (hh + 1) * 128],
                                            mix[:, hh * 512:(hh + 1) * 512],
                                            start=(hh == 0), stop=(hh == HC - 1))
                            if hh == HC - 1:
                                msem.inc(ins, 1)

        @blk.scalar
        def _(sc, s=s):
            sc = CE(sc, schain)
            if s == 0:
                sc.wait_ge(CV.h, 1)
            for t5 in range(NT5):
                for ac in range(HC):
                    itc = (s * NT5 + t5) * HC + ac
                    par = itc % 2
                    sc.wait_ge(RMM.h, itc + 1)
                    if itc >= 2:
                        sc.wait_ge(SDA.h, 2 * (itc - 2) + 2)
                    sc.activation(pb_E[par][:], ps_r[par][:], AF.Exp, scale=-1.0, inc=REV, inc_k=1)
                    sc.wait_ge(SDA.h, itc * 2 + 1)
                    sc.activation(pb_e1s[:], pb_d1[:], AF.Exp, bias=g_w[:, ac:ac + 1])
                    sc.activation(pb_alpha[:], pb_d1[:], AF.Exp)
                    sc.activation(pb_e2s[:], pb_d2[:], AF.Exp, inc=SAD, inc_k=1)
                    sc.wait_ge(SDA.h, itc * 2 + 2)
                    sc.activation(pb_ln[:], pb_D2[:], AF.Ln)
                    sc.activation(pb_r2[:], pb_ln[:], AF.Exp, scale=-1.0, inc=SAD, inc_k=1)


    if stop == "B":
        @blk.sync
        def _(sy):
            UST.wait_all(sy)
        aes.close()
        ES.close()
        return nc

    @blk.vector
    def _(ve):
        ve = CE(ve, vchain)
        ve.nop(inc=BAR)
    @blk.scalar
    def _(sc):
        sc = CE(sc, schain)
        sc.nop(inc=BAR)
    @blk.tensor
    def _(te):
        BAR.inc(te.nop(), 1)
    @blk.sync
    def _(sy):
        UST.wait_all(sy)
        AST.wait_all(sy)
        BAR.inc(sy.nop(), 1)
    @blk.gpsimd
    def _(g):
        BAR.inc(g.nop(), 1)
    bar1 = BAR.n
    aes.close()

    ces = ExitStack()

    def csb(name, shape, dt):
        return ces.enter_context(nc.sbuf_tensor(name, list(shape), dt))

    pc_wo = csb("pc_wo", [128, HC * 2048], BF16)
    pc_u = [csb(f"pc_u{i}", [128, HC * 128], BF16) for i in range(2)]
    pc_x = [csb(f"pc_x{i}", [128, H], F32) for i in range(2)]
    pc_ao = csb("pc_ao", [128, H], F32)
    pc_x1 = [csb(f"pc_x1{i}", [128, H], F32) for i in range(2)]
    pc_b = [csb(f"pc_b{i}", [128, H], BF16) for i in range(2)]
    pc_scr = csb("pc_scr", [128, H], BF16)
    pc_st = csb("pc_st", [128, 24], F32)

    ps_o = [ces.enter_context(nc.psum_tensor(f"ps_o{i}", [128, 512], F32)) for i in range(4)]

    @blk.gpsimd
    def _(g):
        g.wait_ge(BAR.h, bar1)
        for ac in range(HC):
            WOLD.inc(g.dma_start(out=pc_wo[:, ac * 2048:(ac + 1) * 2048],
                                 in_=wo_h[ac * 128:(ac + 1) * 128, :]), 16)

    for s in range(S):
        @blk.sync
        def _(sy, s=s):
            sy.wait_ge(BAR.h, bar1)
            UST.wait_slots(sy, (s + 1) * NT5 * HC)
            def cstores(sy, jt):
                tcj = jt % TC
                sy.wait_ge(CV3.h, jt + 1)
                X1ST.inc(sy.dma_start(out=x1_sc[s, tcj * 128:(tcj + 1) * 128, :],
                                      in_=pc_x1[jt % 2][:]), jt)
                sy.wait_ge(CAPP.h, jt + 1)
                BST.inc(sy.dma_start(out=b_tm[s, tcj * 128:(tcj + 1) * 128, :],
                                     in_=pc_b[jt % 2][:]), jt)
            for tc in range(TC):
                it = s * TC + tc
                if it >= 2:
                    sy.wait_ge(OMM.h, (it - 1) * 4)
                ULD.inc(sy.dma_start(
                    out=pc_u[it % 2][:],
                    in_=u_sc[s, :, :, tc * 128:(tc + 1) * 128].rearrange("a p t -> p a t")), it)
                if it >= 2:
                    sy.wait_ge(CV2.h, it - 1)
                CXL.inc(sy.dma_start(out=pc_x[it % 2][:],
                                     in_=x_hbm[s, tc * 128:(tc + 1) * 128, :]), it)
                if tc >= 1:
                    cstores(sy, it - 1)
            cstores(sy, s * TC + TC - 1)

        @blk.tensor
        def _(te, s=s):
            if s == 0:
                te.wait_ge(BAR.h, bar1)
                te.wait_ge(WOLD.h, HC * 16)
            for tc in range(TC):
                it = s * TC + tc
                ULD.wait_slots(te, it + 1)
                for h5 in range(4):
                    jt = it * 4 + h5
                    if jt >= 4:
                        te.wait_ge(OEV.h, jt - 3)
                    for ac in range(HC):
                        ins = te.matmul(ps_o[jt % 4][:], pc_u[it % 2][:, ac * 128:(ac + 1) * 128],
                                        pc_wo[:, ac * 2048 + h5 * 512:ac * 2048 + (h5 + 1) * 512],
                                        start=(ac == 0), stop=(ac == HC - 1))
                        if ac == HC - 1:
                            OMM.inc(ins, 1)

        @blk.vector
        def _(ve, s=s):
            ve = CE(ve, vchain)
            if s == 0:
                ve.wait_ge(BAR.h, bar1)
            for tc in range(TC):
                it = s * TC + tc
                if it >= 1:
                    ve.wait_ge(OSQ.h, it * 4)
                for h5 in range(4):
                    jt = it * 4 + h5
                    ve.wait_ge(OMM.h, jt + 1)
                    ve.tensor_scalar(pc_ao[:, h5 * 512:(h5 + 1) * 512], ps_o[jt % 4][:],
                                             1.0, None, ALU.mult, ALU.add,
                                             accum_out=pc_st[:, h5:h5 + 1], inc=OEV, inc_k=1)
                ve.wait_ge(OSQ.h, (it + 1) * 4)
                ve.tensor_tensor(pc_st[:, 8:9], pc_st[:, 0:1], pc_st[:, 1:2], ALU.add)
                ve.tensor_tensor(pc_st[:, 9:10], pc_st[:, 2:3], pc_st[:, 3:4], ALU.add)
                ve.tensor_tensor(pc_st[:, 8:9], pc_st[:, 8:9], pc_st[:, 9:10], ALU.add)
                ve.tensor_tensor(pc_st[:, 10:11], pc_st[:, 4:5], pc_st[:, 5:6], ALU.add)
                ve.tensor_tensor(pc_st[:, 11:12], pc_st[:, 6:7], pc_st[:, 7:8], ALU.add)
                ve.tensor_tensor(pc_st[:, 10:11], pc_st[:, 10:11], pc_st[:, 11:12], ALU.add)
                ve.tensor_scalar(pc_st[:, 8:9], pc_st[:, 8:9], 1.0 / H, None, ALU.mult, ALU.bypass)
                ve.tensor_scalar(pc_st[:, 10:11], pc_st[:, 10:11], 1.0 / H, None, ALU.mult, ALU.bypass)
                ve.tensor_tensor(pc_st[:, 9:10], pc_st[:, 8:9], pc_st[:, 8:9], ALU.mult)
                ve.tensor_tensor(pc_st[:, 9:10], pc_st[:, 10:11], pc_st[:, 9:10],
                                         ALU.subtract, inc=CV1, inc_k=1)
                ve.wait_ge(CS1.h, it + 1)
                ve.scalar_tensor_tensor(pc_st[:, 13:14], pc_st[:, 8:9], -1.0, pc_st[:, 12:13],
                                        ALU.mult, ALU.mult)
                CXL.wait_slots(ve, it + 1)
                x1t = pc_x1[it % 2]
                if it >= 2:
                    ve.wait_ge(CAPP.h, it - 1)
                    X1ST.wait_slots(ve, it - 1)
                ve.tensor_scalar(x1t[:], pc_x[it % 2][:], pc_st[:, 13:14], None, ALU.add, ALU.bypass)
                ve.scalar_tensor_tensor(x1t[:], pc_ao[:], pc_st[:, 12:13], x1t[:],
                                                ALU.mult, ALU.add, inc=CV2, inc_k=1)
                ve.tensor_scalar(pc_scr[:], x1t[:], 1.0, None, ALU.mult, ALU.add,
                                         accum_out=pc_st[:, 14:15], inc=CV3, inc_k=1)
                ve.wait_ge(CS2.h, it + 1)
                ve.tensor_scalar(pc_st[:, 16:17], pc_st[:, 14:15], 1.0 / H, None, ALU.mult, ALU.bypass)
                ve.tensor_scalar(pc_st[:, 17:18], pc_st[:, 15:16], 1.0 / H, None, ALU.mult, ALU.bypass)
                ve.tensor_tensor(pc_st[:, 18:19], pc_st[:, 16:17], pc_st[:, 16:17], ALU.mult)
                ve.tensor_tensor(pc_st[:, 18:19], pc_st[:, 17:18], pc_st[:, 18:19],
                                         ALU.subtract, inc=CV4, inc_k=1)
                ve.wait_ge(CS3.h, it + 1)
                ve.scalar_tensor_tensor(pc_st[:, 20:21], pc_st[:, 16:17], -1.0,
                                                pc_st[:, 19:20], ALU.mult, ALU.mult, inc=CV1, inc_k=1)

        @blk.scalar
        def _(sc, s=s):
            sc = CE(sc, schain)
            if s == 0:
                sc.wait_ge(BAR.h, bar1)
            for tc in range(TC):
                it = s * TC + tc
                for h5 in range(4):
                    jt = it * 4 + h5
                    sc.wait_ge(OEV.h, jt + 1)
                    sc.activation(pc_scr[:, h5 * 512:(h5 + 1) * 512],
                                          pc_ao[:, h5 * 512:(h5 + 1) * 512], AF.Square,
                                          accum_out=pc_st[:, 4 + h5:5 + h5], inc=OSQ, inc_k=1)
                sc.wait_ge(CV1.h, 2 * it + 1)
                sc.activation(pc_st[:, 11:12], pc_st[:, 9:10], AF.Ln, bias=g_eps[:, 0:1])
                sc.activation(pc_st[:, 12:13], pc_st[:, 11:12], AF.Exp, scale=-0.5, inc=CS1, inc_k=1)
                sc.wait_ge(CV3.h, it + 1)
                sc.activation(pc_scr[:], pc_x1[it % 2][:], AF.Square,
                                      accum_out=pc_st[:, 15:16], inc=CS2, inc_k=1)
                sc.wait_ge(CV4.h, it + 1)
                sc.activation(pc_st[:, 21:22], pc_st[:, 18:19], AF.Ln, bias=g_eps[:, 0:1])
                sc.activation(pc_st[:, 19:20], pc_st[:, 21:22], AF.Exp, scale=-0.5, inc=CS3, inc_k=1)
                sc.wait_ge(CV1.h, 2 * it + 2)
                if it >= 2:
                    BST.wait_slots(sc, it - 1)
                sc.activation(pc_b[it % 2][:], pc_x1[it % 2][:], AF.Identity,
                                       bias=pc_st[:, 20:21], scale=pc_st[:, 19:20], inc=CAPP, inc_k=1)


    if stop == "C":
        @blk.sync
        def _(sy):
            X1ST.wait_all(sy)
            BST.wait_all(sy)
        ces.close()
        ES.close()
        return nc

    @blk.vector
    def _(ve):
        ve = CE(ve, vchain)
        ve.nop(inc=BAR)
    @blk.scalar
    def _(sc):
        sc = CE(sc, schain)
        sc.nop(inc=BAR)
    @blk.tensor
    def _(te):
        BAR.inc(te.nop(), 1)
    @blk.sync
    def _(sy):
        X1ST.wait_all(sy)
        BST.wait_all(sy)
        UST.wait_all(sy)
        BAR.inc(sy.nop(), 1)
    @blk.gpsimd
    def _(g):
        BAR.inc(g.nop(), 1)
    bar0 = BAR.n
    ces.close()

    NSLOT1 = HC * NT5
    DELAY = 2
    NRING_E = 9

    fes = ExitStack()

    def psb(name, shape, dt):
        return fes.enter_context(nc.sbuf_tensor(name, list(shape), dt))

    b_fm = psb("b_fm", [128, HC * 2048], BF16)
    c_sb = psb("c_sb", [128, 128], F32)
    cT_sb = [psb(f"cT{i}", [8, 128], F32) for i in range(TC)]
    fsum_sb = psb("fsum_sb", [128, TC * 4], F32)
    fsq_sb = psb("fsq_sb", [128, TC * 4], F32)
    ps_c = fes.enter_context(nc.psum_tensor("ps_c", [128, 512], F32))
    ps_L = [fes.enter_context(nc.psum_tensor(f"ps_L{i}", [128, 512], F32)) for i in range(2)]
    ps_B = [fes.enter_context(nc.psum_tensor(f"ps_B{i}", [128, 512], F32)) for i in range(2)]
    ps_LX = ps_L + ps_B
    ps_Y = [fes.enter_context(nc.psum_tensor(f"ps_Y{i}", [128, 512], F32)) for i in range(2)]

    BFL = sem("BFL")
    RWL = DSem(ES, nc, "RWL", nway=NRING_E)
    LMM = sem("LMM"); LEVA = sem("LEVA")
    EV1 = sem("EV1"); EA1 = sem("EA1"); EV2 = sem("EV2"); EA2 = sem("EA2")
    CTR = sem("CTR"); CMM = sem("CMM"); CEV = sem("CEV")
    WEL = DSem(ES, nc, "WEL", nway=3)
    BMM = sem("BMM"); BEVA = sem("BEVA"); CTEV = sem("CTEV")
    YMM = sem("YMM"); YEV = sem("YEV"); FST = sem("FST"); FSQ = sem("FSQ")
    TPD = sem("TPD"); EBL = sem("EBL")
    GXL = DSem(ES, nc, "GXL", m=2)
    GV1 = sem("GV1"); GS1 = sem("GS1")
    OUTS = DSem(ES, nc, "OUTS")
    FFST = sem("FFST")

    def emit_barrier():
        @blk.vector
        def _(ve):
            ve = CE(ve, vchain)
            ve.nop(inc=BAR)
        @blk.scalar
        def _(sc):
            sc = CE(sc, schain)
            sc.nop(inc=BAR)
        @blk.tensor
        def _(te):
            BAR.inc(te.nop(), 1)
        @blk.sync
        def _(sy):
            UST.wait_all(sy)
            X1ST.wait_all(sy)
            BST.wait_all(sy)
            sy.wait_ge(FFST.h, FFST.n)
            OUTS.wait_all(sy)
            BAR.inc(sy.nop(), 1)
        @blk.gpsimd
        def _(g):
            BAR.inc(g.nop(), 1)
        return BAR.n

    def emit_cmm(te, fl):
        te.wait_ge(CTR.h, fl + 1)
        if fl % NSLOT1 == 0 and fl > 0:
            te.wait_ge(CEV.h, fl // NSLOT1)
        ctr = ctr_sb[fl % 2]
        t5f = fl % NT5
        for e in range(E):
            for j in range(4):
                tcf = t5f * 4 + j
                col = tcf * E + e
                ins = te.matmul(ps_c[:, col:col + 1],
                                ctr[:, e * 512 + j * 128:e * 512 + (j + 1) * 128],
                                g_onescol[:],
                                start=(fl % NSLOT1 == 0 and e == 0 and j == 0),
                                stop=(fl % NSLOT1 == NSLOT1 - 1 and e == E - 1 and j == 3),
                                skip_group_check=True)
                if e == E - 1 and j == 3:
                    CMM.inc(ins, 1)

    for s in range(S):
        ees = ExitStack()

        def esb(name, shape, dt):
            return ees.enter_context(nc.sbuf_tensor(name, list(shape), dt))

        wr_ring = [esb(f"wr_ring{s}_{i}", [128, 2048], BF16) for i in range(NRING_E)]
        L_sb = [esb(f"L_sb{s}_{i}", [128, E * 512], F32) for i in range(2)]
        ctr_sb = [esb(f"ctr{s}_{i}", [128, E * 512], BF16) for i in range(2)]
        d8_sb = esb(f"d8_sb{s}", [128, E * 512], F32)
        x8_sb = esb(f"x8_sb{s}", [128, E * 512], BF16)
        m1_sb = esb(f"m1_sb{s}", [128, 512], F32)
        m2_sb = esb(f"m2_sb{s}", [128, 512], F32)
        mt_sb = esb(f"mt_sb{s}", [128, 512], F32)
        z_sb = esb(f"z_sb{s}", [128, 512], F32)
        msk_sb = esb(f"msk_sb{s}", [128, 512], BF16)

        bar_e = emit_barrier()

        @blk.scalar
        def _(sc, s=s, bar_e=bar_e):
            sc = CE(sc, schain)
            sc.wait_ge(BAR.h, bar_e)
            BST.wait_slots(sc, (s + 1) * TC)
            for hh in range(HC):
                BFL.inc(sc.dma_start(out=b_fm[:, hh * 2048:(hh + 1) * 2048],
                                     in_=b_tm[s, :, hh * 128:(hh + 1) * 128],
                                     transpose=True), 16)

        @blk.gpsimd
        def _(g, s=s, bar_e=bar_e):
            g.wait_ge(BAR.h, bar_e)
            for oc in range(HC):
                for e in range(E):
                    gidx = s * HC * E + oc * E + e
                    pidx = gidx - NRING_E
                    if pidx >= 0:
                        s_p = pidx // (HC * E)
                        oc_p = (pidx % (HC * E)) // E
                        e_p = pidx % E
                        g.wait_ge(LMM.h, ((s_p * HC + oc_p) * NT5 + 3) * E + e_p + 1)
                    RWL.inc(g.dma_start(out=wr_ring[gidx % NRING_E][:],
                                        in_=wrt_h[e, oc, :, :, :]), gidx)

        @blk.tensor
        def _(te, s=s, bar_e=bar_e):
            te.wait_ge(BAR.h, bar_e)
            te.wait_ge(BFL.h, (s + 1) * HC * 16)
            pend = []
            for oc in range(HC):
                for t5 in range(NT5):
                    slot = (s * HC + oc) * NT5 + t5
                    for e in range(E):
                        idx = (s * HC + oc) * E + e
                        RWL.wait_slots(te, idx + 1)
                        gidx = slot * E + e
                        if gidx - s * NSLOT1 * E >= 4:
                            te.wait_ge(LEVA.h, gidx - 3)
                        wr_t = wr_ring[idx % NRING_E]
                        for hh in range(HC):
                            ins = te.matmul(ps_LX[gidx % 4][:], wr_t[:, hh * 128:(hh + 1) * 128],
                                            b_fm[:, hh * 2048 + t5 * 512:hh * 2048 + (t5 + 1) * 512],
                                            start=(hh == 0), stop=(hh == HC - 1))
                            if hh == HC - 1:
                                LMM.inc(ins, 1)
                    pend.append(slot)
                    if len(pend) > DELAY:
                        emit_cmm(te, pend.pop(0))
            for fl in pend:
                emit_cmm(te, fl)

        @blk.scalar
        def _(sc, s=s):
            sc = CE(sc, schain)
            for oc in range(HC):
                for t5 in range(NT5):
                    slot = (s * HC + oc) * NT5 + t5
                    par = slot % 2
                    if slot - s * NSLOT1 >= 2:
                        sc.wait_ge(CTR.h, slot - 1)
                    for e in range(E):
                        sc.wait_ge(LMM.h, slot * E + e + 1)
                        sc.activation(L_sb[par][:, e * 512:(e + 1) * 512],
                                               ps_LX[(slot * E + e) % 4][:], AF.Identity,
                                               bias=g_rb[:, e * 16 + oc:e * 16 + oc + 1], inc=LEVA, inc_k=1)
                    sc.wait_ge(EV1.h, slot + 1)
                    sc.activation(z_sb[:], mt_sb[:], AF.Exp)
                    sc.activation(mt_sb[:], z_sb[:], AF.Ln, bias=1.0, inc=EA1, inc_k=1)
                    sc.wait_ge(EV2.h, slot + 1)
                    for e in range(E):
                        sc.activation(x8_sb[:, e * 512:(e + 1) * 512],
                                      d8_sb[:, e * 512:(e + 1) * 512], AF.Exp,
                                      inc=EA2 if e == E - 1 else None)

        @blk.vector
        def _(ve, s=s):
            ve = CE(ve, vchain)
            for oc in range(HC):
                for t5 in range(NT5):
                    slot = (s * HC + oc) * NT5 + t5
                    par = slot % 2
                    L = L_sb[par]
                    ve.wait_ge(LEVA.h, (slot + 1) * E)
                    ve.tensor_tensor(m1_sb[:], L[:, 0:512], L[:, 512:1024], ALU.max)
                    ve.tensor_tensor(m2_sb[:], L[:, 0:512], L[:, 512:1024], ALU.min)
                    for e in range(2, E):
                        Le = L[:, e * 512:(e + 1) * 512]
                        ve.tensor_tensor(mt_sb[:], m1_sb[:], Le, ALU.min)
                        ve.tensor_tensor(m2_sb[:], m2_sb[:], mt_sb[:], ALU.max)
                        ve.tensor_tensor(m1_sb[:], m1_sb[:], Le, ALU.max)
                    ve.tensor_tensor(mt_sb[:], m2_sb[:], m1_sb[:], ALU.subtract, inc=EV1, inc_k=1)
                    ve.wait_ge(EA1.h, slot + 1)
                    ve.tensor_tensor(z_sb[:], m1_sb[:], mt_sb[:], ALU.add)
                    for e in range(E):
                        ve.tensor_tensor(d8_sb[:, e * 512:(e + 1) * 512],
                                         L[:, e * 512:(e + 1) * 512], z_sb[:], ALU.subtract,
                                         inc=EV2 if e == E - 1 else None)
                    ve.wait_ge(EA2.h, slot + 1)
                    if slot - s * NSLOT1 >= 2:
                        ve.wait_ge(CMM.h, slot - 1)
                    ctr = ctr_sb[par]
                    for e in range(E):
                        ve.tensor_tensor(msk_sb[:], L[:, e * 512:(e + 1) * 512], m2_sb[:],
                                         ALU.is_ge)
                        ve.tensor_tensor(ctr[:, e * 512:(e + 1) * 512],
                                         x8_sb[:, e * 512:(e + 1) * 512], msk_sb[:], ALU.mult,
                                         inc=CTR if e == E - 1 else None)

        @blk.vector
        def _(ve, s=s):
            ve = CE(ve, vchain)
            ve.wait_ge(CMM.h, (s + 1) * NSLOT1)
            ve.tensor_scalar(c_sb[:], ps_c[:, 0:128], 1.0, None, ALU.mult, ALU.bypass, inc=CEV, inc_k=1)

        if c_dbg is not None:
            @blk.sync
            def _(sy, s=s):
                sy.wait_ge(CEV.h, s + 1)
                DBGS.inc(sy.dma_start(out=c_dbg[s, :, :], in_=c_sb[:]), 16)

        if stop == "E":
            @blk.sync
            def _(sy, s=s):
                sy.wait_ge(CEV.h, s + 1)
                if DBGS.n:
                    sy.wait_ge(DBGS.h, DBGS.n)
            ees.close()
            fes.close()
            ES.close()
            return nc

        bar_f = emit_barrier()
        ees.close()

        ffs = ExitStack()

        def xsb(name, shape, dt):
            return ffs.enter_context(nc.sbuf_tensor(name, list(shape), dt))

        we_ring = [xsb(f"we_ring{s}_{i}", [128, HC * 512], BF16) for i in range(3)]
        acc_sb = [xsb(f"acc{s}_{i}", [128, 512], F32) for i in range(TC)]
        fscr_sb = xsb(f"fscr_sb{s}", [128, 512], BF16)
        f_eb = xsb(f"f_eb{s}", [8, O], F32)
        f_eye = xsb(f"f_eye{s}", [128, 128], F32)

        @blk.sync
        def _(sy, s=s, bar_f=bar_f):
            sy.wait_ge(BAR.h, bar_f)
            EBL.inc(sy.dma_start(out=f_eb[:], in_=eb_h[:, :]), 16)
            EBL.inc(sy.dma_start(out=f_eye[:], in_=eye_h[:, :]), 16)

        @blk.tensor
        def _(te, s=s, bar_f=bar_f):
            te.wait_ge(BAR.h, bar_f)
            te.wait_ge(CEV.h, s + 1)
            te.wait_ge(EBL.h, (s + 1) * 32)
            for tc in range(TC):
                if tc >= 1:
                    te.wait_ge(CTEV.h, s * TC + tc)
                TPD.inc(te.transpose(ps_B[0][0:8, 0:128], c_sb[:, tc * E:(tc + 1) * E],
                                     f_eye[:]), 1)

        @blk.vector
        def _(ve, s=s):
            ve = CE(ve, vchain)
            for tc in range(TC):
                ve.wait_ge(TPD.h, s * TC + tc + 1)
                ve.tensor_scalar(cT_sb[tc][:], ps_B[0][0:8, 0:128], 1.0, None,
                                          ALU.mult, ALU.bypass, inc=CTEV, inc_k=1)

        @blk.gpsimd
        def _(g, s=s, bar_f=bar_f):
            g.wait_ge(BAR.h, bar_f)
            for oc4 in range(4):
                for e in range(E):
                    gidx = s * 4 * E + oc4 * E + e
                    if gidx >= 3:
                        g.wait_ge(YMM.h, (gidx - 3 + 1) * TC)
                    WEL.inc(g.dma_start(out=we_ring[gidx % 3][:], in_=we_h[e, oc4, :, :, :]), gidx)

        @blk.tensor
        def _(te, s=s):
            te.wait_ge(CTEV.h, (s + 1) * TC)
            for oc4 in range(4):
                for tc in range(TC):
                    jt = (s * 4 + oc4) * TC + tc
                    if jt - s * 4 * TC >= 2:
                        te.wait_ge(BEVA.h, jt - 1)
                    BMM.inc(te.matmul(ps_B[jt % 2][:], cT_sb[tc][:],
                                      f_eb[0:8, oc4 * 512:(oc4 + 1) * 512],
                                      start=True, stop=True), 1)
                for e in range(E):
                    idx = (s * 4 + oc4) * E + e
                    WEL.wait_slots(te, idx + 1)
                    wet = we_ring[idx % 3]
                    for tc in range(TC):
                        jt = idx * TC + tc
                        if jt - s * 4 * E * TC >= 2:
                            te.wait_ge(YEV.h, jt - 1)
                        for hh in range(HC):
                            ins = te.matmul(ps_Y[jt % 2][:],
                                            b_fm[:, hh * 2048 + tc * 128:hh * 2048 + (tc + 1) * 128],
                                            wet[:, hh * 512:(hh + 1) * 512],
                                            start=(hh == 0), stop=(hh == HC - 1))
                            if hh == HC - 1:
                                YMM.inc(ins, 1)

        @blk.vector
        def _(ve, s=s):
            ve = CE(ve, vchain)
            for oc4 in range(4):
                for tc in range(TC):
                    jt = (s * 4 + oc4) * TC + tc
                    ve.wait_ge(BMM.h, jt + 1)
                    if oc4 >= 1:
                        ve.wait_ge(FSQ.h, jt - TC + 1)
                        ve.wait_ge(FFST.h, (s * 4 + oc4) * TC * 16)
                    ve.tensor_scalar(acc_sb[tc][:], ps_B[jt % 2][:], 1.0, None,
                                              ALU.mult, ALU.bypass, inc=BEVA, inc_k=1)
                for e in range(E):
                    idx = (s * 4 + oc4) * E + e
                    for tc in range(TC):
                        jt = idx * TC + tc
                        ve.wait_ge(YMM.h, jt + 1)
                        ve.scalar_tensor_tensor(acc_sb[tc][:], ps_Y[jt % 2][:],
                                                        c_sb[:, tc * E + e:tc * E + e + 1],
                                                        acc_sb[tc][:], ALU.mult, ALU.add, inc=YEV, inc_k=1)
                for tc in range(TC):
                    col = tc * 4 + oc4
                    ve.tensor_scalar(fscr_sb[:], acc_sb[tc][:], 1.0, None, ALU.mult,
                                             ALU.add, accum_out=fsum_sb[:, col:col + 1], inc=FST, inc_k=1)

        @blk.scalar
        def _(sc, s=s):
            sc = CE(sc, schain)
            for oc4 in range(4):
                for tc in range(TC):
                    jt = (s * 4 + oc4) * TC + tc
                    col = tc * 4 + oc4
                    sc.wait_ge(FST.h, jt + 1)
                    sc.activation(fscr_sb[:], acc_sb[tc][:], AF.Square,
                                          accum_out=fsq_sb[:, col:col + 1], inc=FSQ, inc_k=1)

        @blk.sync
        def _(sy, s=s):
            for oc4 in range(4):
                for tc in range(TC):
                    jt = (s * 4 + oc4) * TC + tc
                    sy.wait_ge(FSQ.h, jt + 1)
                    FFST.inc(sy.dma_start(out=ffn_sc[s, tc, :, oc4 * 512:(oc4 + 1) * 512],
                                          in_=acc_sb[tc][:]), 16)

        if stop == "F":
            @blk.sync
            def _(sy, s=s):
                sy.wait_ge(FFST.h, FFST.n)
                if DBGS.n:
                    sy.wait_ge(DBGS.h, DBGS.n)
            ffs.close()
            fes.close()
            ES.close()
            return nc

        bar_g = emit_barrier()
        ffs.close()

        ggs = ExitStack()

        def gsb(name, shape, dt):
            return ggs.enter_context(nc.sbuf_tensor(name, list(shape), dt))

        pg_x1 = [gsb(f"pg_x1{s}_{i}", [128, H], F32) for i in range(2)]
        pg_f = [gsb(f"pg_f{s}_{i}", [128, H], F32) for i in range(2)]
        pg_o = [gsb(f"pg_o{s}_{i}", [128, H], F32) for i in range(2)]
        pg_st = gsb(f"pg_st{s}", [128, 8], F32)

        @blk.sync
        def _(sy, s=s, bar_g=bar_g):
            sy.wait_ge(BAR.h, bar_g)
            for tc in range(TC):
                it = s * TC + tc
                if tc >= 2:
                    sy.wait_ge(GV1.h, 2 * (it - 2) + 2)
                GXL.inc(sy.dma_start(out=pg_x1[it % 2][:],
                                     in_=x1_sc[s, tc * 128:(tc + 1) * 128, :]), it)
                GXL.inc(sy.dma_start(out=pg_f[it % 2][:], in_=ffn_sc[s, tc, :, :]), it)
                if tc >= 1:
                    jt = it - 1
                    sy.wait_ge(GV1.h, 2 * jt + 2)
                    OUTS.inc(sy.dma_start(
                        out=out_hbm[s, (jt % TC) * 128:((jt % TC) + 1) * 128, :],
                        in_=pg_o[jt % 2][:]), jt)
            jt = (s + 1) * TC - 1
            sy.wait_ge(GV1.h, 2 * jt + 2)
            OUTS.inc(sy.dma_start(
                out=out_hbm[s, (jt % TC) * 128:((jt % TC) + 1) * 128, :],
                in_=pg_o[jt % 2][:]), jt)

        @blk.vector
        def _(ve, s=s):
            ve = CE(ve, vchain)
            ve.wait_ge(FSQ.h, (s + 1) * 4 * TC)
            for tc in range(TC):
                it = s * TC + tc
                GXL.wait_slots(ve, it + 1)
                ve.tensor_tensor(pg_st[:, 0:1], fsum_sb[:, tc * 4:tc * 4 + 1],
                                 fsum_sb[:, tc * 4 + 1:tc * 4 + 2], ALU.add)
                ve.tensor_tensor(pg_st[:, 1:2], fsum_sb[:, tc * 4 + 2:tc * 4 + 3],
                                 fsum_sb[:, tc * 4 + 3:tc * 4 + 4], ALU.add)
                ve.tensor_tensor(pg_st[:, 0:1], pg_st[:, 0:1], pg_st[:, 1:2], ALU.add)
                ve.tensor_tensor(pg_st[:, 2:3], fsq_sb[:, tc * 4:tc * 4 + 1],
                                 fsq_sb[:, tc * 4 + 1:tc * 4 + 2], ALU.add)
                ve.tensor_tensor(pg_st[:, 3:4], fsq_sb[:, tc * 4 + 2:tc * 4 + 3],
                                 fsq_sb[:, tc * 4 + 3:tc * 4 + 4], ALU.add)
                ve.tensor_tensor(pg_st[:, 2:3], pg_st[:, 2:3], pg_st[:, 3:4], ALU.add)
                ve.tensor_scalar(pg_st[:, 0:1], pg_st[:, 0:1], 1.0 / O, None, ALU.mult, ALU.bypass)
                ve.tensor_scalar(pg_st[:, 2:3], pg_st[:, 2:3], 1.0 / O, None, ALU.mult, ALU.bypass)
                ve.tensor_tensor(pg_st[:, 1:2], pg_st[:, 0:1], pg_st[:, 0:1], ALU.mult)
                ve.tensor_tensor(pg_st[:, 1:2], pg_st[:, 2:3], pg_st[:, 1:2],
                                         ALU.subtract, inc=GV1, inc_k=1)
                ve.wait_ge(GS1.h, it + 1)
                ve.scalar_tensor_tensor(pg_st[:, 4:5], pg_st[:, 0:1], -1.0, pg_st[:, 5:6],
                                        ALU.mult, ALU.mult)
                ot = pg_o[it % 2]
                if tc >= 2:
                    OUTS.wait_slots(ve, it - 1)
                ve.tensor_scalar(ot[:], pg_x1[it % 2][:], pg_st[:, 4:5], None, ALU.add, ALU.bypass)
                ve.scalar_tensor_tensor(ot[:], pg_f[it % 2][:], pg_st[:, 5:6], ot[:],
                                                ALU.mult, ALU.add, inc=GV1, inc_k=1)

        @blk.scalar
        def _(sc, s=s):
            sc = CE(sc, schain)
            for tc in range(TC):
                it = s * TC + tc
                sc.wait_ge(GV1.h, 2 * it + 1)
                sc.activation(pg_st[:, 6:7], pg_st[:, 1:2], AF.Ln, bias=g_eps[:, 0:1])
                sc.activation(pg_st[:, 5:6], pg_st[:, 6:7], AF.Exp, scale=-0.5, inc=GS1, inc_k=1)

        bar_end = emit_barrier()
        ggs.close()

    @blk.sync
    def _(sy):
        OUTS.wait_all(sy)
        if DBGS.n:
            sy.wait_ge(DBGS.h, DBGS.n)

    fes.close()
    ES.close()
    return nc


def _pack_weights(L0_attn_time_decay, L0_attn_time_first, L0_attn_time_mix_key,
                  L0_attn_time_mix_value, L0_attn_time_mix_receptance, L0_attn_W_key,
                  L0_attn_W_value, L0_attn_W_receptance, L0_attn_W_output,
                  L0_ffn_router_w, L0_ffn_router_b, L0_ffn_expert_w, L0_ffn_expert_b):
    C = np.ascontiguousarray

    def kv_pack(w):
        return C(np.asarray(w, np.float32).T.reshape(HC, 128, HC, 128).transpose(2, 1, 0, 3))

    def vec_pack(v):
        return C(np.asarray(v, np.float32).reshape(-1).reshape(HC, 128).T)

    wrt = np.asarray(L0_ffn_router_w, np.float32).reshape(O, E, H).transpose(1, 2, 0)
    wrt = C(wrt.reshape(E, HC, 128, HC, 128).transpose(0, 3, 2, 1, 4))
    rb = np.asarray(L0_ffn_router_b, np.float32).reshape(O, E).T
    rb = C(rb.reshape(E, 16, 128).transpose(2, 0, 1).reshape(128, 128))
    we = np.asarray(L0_ffn_expert_w, np.float32).transpose(0, 2, 1)
    we = C(we.reshape(E, HC, 128, 4, 512).transpose(0, 3, 2, 1, 4))
    return {
        "wk": kv_pack(L0_attn_W_key),
        "wv": kv_pack(L0_attn_W_value),
        "wr": kv_pack(L0_attn_W_receptance),
        "wo": C(np.asarray(L0_attn_W_output, np.float32).T),
        "wrt": wrt,
        "rb": rb,
        "we": we,
        "eb": C(np.asarray(L0_ffn_expert_b, np.float32)),
        "td": vec_pack(L0_attn_time_decay),
        "tf": vec_pack(L0_attn_time_first),
        "tmk": vec_pack(L0_attn_time_mix_key),
        "tmv": vec_pack(L0_attn_time_mix_value),
        "tmr": vec_pack(L0_attn_time_mix_receptance),
        "eye": np.eye(128, dtype=np.float32),
    }


def kernel(**inputs):
    x = np.asarray(inputs["L0_attn_hidden"], np.float32)
    shared = _pack_weights(**{k: v for k, v in inputs.items() if k != "L0_attn_hidden"})
    nc = build(dbg=DBG)
    in_maps = []
    for c in range(NCORES):
        m = dict(shared)
        m["x"] = np.ascontiguousarray(x[c * S:(c + 1) * S])
        in_maps.append(m)
    res = run_bass_kernel_spmd(nc, in_maps, core_ids=list(range(NCORES)))
    out = np.concatenate([np.asarray(r["out"]) for r in res.results], axis=0)
    return out
